# version 32
# baseline (speedup 1.0000x reference)
import os
import numpy as np
import concourse.bass as bass
import concourse.mybir as mybir
from concourse.bass_utils import run_bass_kernel_spmd

F32 = mybir.dt.float32
BF16 = mybir.dt.bfloat16
I8 = mybir.dt.int8
I32 = mybir.dt.int32
AF = mybir.ActivationFunctionType
ALU = mybir.AluOpType
AX = mybir.AxisListType

ENGS = ("sync", "vector", "scalar", "gpsimd", "tensor")


def build_nc(S, D, H):
    E = D
    TOK = S // 4
    NB = TOK // 128
    KC = D // 128
    CC = E // 128
    NPG = CC // 2
    GT = min(512, S // 4)
    NG = (S // 4) // GT
    GB = GT // 128
    NSL = H // 512
    HK = H // 128
    HKG = HK // 8
    NDS = max(1, D // 1024)
    DW = min(512, D)
    WSF = 1024

    nc = bass.Bass("TRN2", target_bir_lowering=False)

    SHD_W = D // 8
    SHD_D = H // 8
    xo_d = nc.dram_tensor("xown", [TOK, D], F32, kind="ExternalInput")
    wtsh_d = nc.dram_tensor("wtsh", [SHD_W, 2 * E], F32, kind="ExternalInput")
    wgsh_d = nc.dram_tensor("wgsh", [SHD_W, H], F32, kind="ExternalInput")
    wush_d = nc.dram_tensor("wush", [SHD_W, H], F32, kind="ExternalInput")
    wdsh_d = nc.dram_tensor("wdsh", [SHD_D, D], F32, kind="ExternalInput")
    cm_d = nc.dram_tensor("cmask", [128, 16], F32, kind="ExternalInput")
    out_d = nc.dram_tensor("out", [TOK, D], F32, kind="ExternalOutput")
    wtbfsh_d = nc.dram_tensor("wtbfsh", [SHD_W, 2 * E], BF16, kind="Internal")
    wgqsh_d = nc.dram_tensor("wgqsh", [SHD_W, H], BF16, kind="Internal")
    wuqsh_d = nc.dram_tensor("wuqsh", [SHD_W, H], BF16, kind="Internal")
    wdqsh_d = nc.dram_tensor("wdqsh", [SHD_D, D], BF16, kind="Internal")
    wtbf_d = nc.dram_tensor("wtbf", [D, 2 * E], BF16, kind="Internal", addr_space="Shared")
    wgq_d = nc.dram_tensor("wgq", [D, H], BF16, kind="Internal", addr_space="Shared")
    wuq_d = nc.dram_tensor("wuq", [D, H], BF16, kind="Internal", addr_space="Shared")
    wdq_d = nc.dram_tensor("wdq_g", [H, D], BF16, kind="Internal", addr_space="Shared")
    s_d = nc.dram_tensor("s_scr", [TOK, E], F32, kind="Internal")
    p_d = nc.dram_tensor("p_scr", [TOK, E], F32, kind="Internal")
    ccin_d = nc.dram_tensor("cc_in", [128, 32], F32, kind="Internal")
    ccout_d = nc.dram_tensor("cc_out", [8 * 128, 32], F32, kind="Internal", addr_space="Shared")
    h_d = nc.dram_tensor("h_scr", [TOK, H], F32, kind="Internal")
    ih_d = nc.dram_tensor("ih_scr", [HKG * NB * 128, 8 * 128], BF16, kind="Internal")

    xo = xo_d.ap(); wtsh = wtsh_d.ap(); wgsh = wgsh_d.ap()
    wush = wush_d.ap(); wdsh = wdsh_d.ap(); cm = cm_d.ap(); out = out_d.ap()
    wtbf = wtbf_d.ap(); s_a = s_d.ap(); h_a = h_d.ap(); ih_a = ih_d.ap()
    p_a = p_d.ap(); wgq = wgq_d.ap(); wuq = wuq_d.ap(); wdqa = wdq_d.ap()
    wtbfsh = wtbfsh_d.ap(); wgqsh = wgqsh_d.ap(); wuqsh = wuqsh_d.ap()
    wdqsh = wdqsh_d.ap()

    def sb(name, shape, dt):
        return nc.alloc_sbuf_tensor(name, shape, dt)

    ident = sb("ident", [128, 128], BF16)
    identf = sb("identf", [128, 128], F32)
    identi = sb("identi", [128, 128], I32)
    ones_col = sb("ones_col", [128, 1], F32)
    ones_row = sb("ones_row", [1, 128], F32)
    c5 = sb("c5", [128, 1], F32)
    c6 = sb("c6", [128, 1], F32)
    cm_sb = sb("cm_sb", [128, 16], F32)
    Hcar = sb("Hcar", [128, 16], F32)
    Htr = sb("Htr", [16, 128], F32)
    Rrow = sb("Rrow", [1, 128], F32)
    cstg = sb("cstg", [128, 32], F32)
    ccg = sb("ccg", [128, 256], F32)
    tmpH = sb("tmpH", [128, 16], F32); tmpP = sb("tmpP", [128, 16], F32)
    tmpQ = sb("tmpQ", [128, 16], F32)
    carryP = sb("carryP", [128, 16], F32)
    STAT = lambda n: sb(n, [128, max(NB, 4)], F32)
    musum = STAT("musum"); sumsq = STAT("sumsq"); maxx = STAT("maxx"); minn = STAT("minn")
    mu = STAT("mu"); rho = STAT("rho"); rr = STAT("rr"); alpha = STAT("alpha")
    sinv = STAT("sinv"); tmpa = STAT("tmpa"); tmpb = STAT("tmpb"); tmpc = STAT("tmpc")
    sumsqh = STAT("sumsqh"); maxh = STAT("maxh"); alphah = STAT("alphah"); sinvh = STAT("sinvh")
    hsq1 = STAT("hsq1"); hsq2 = STAT("hsq2")
    scg = STAT("scg"); scu = STAT("scu"); scd = STAT("scd")
    carry = sb("carry", [128, 16], F32)
    wacc = sb("wacc", [128, 64], F32)
    wbar = sb("wbar", [128, 4], F32); wclip = sb("wclip", [128, 4], F32)
    winv = sb("winv", [128, 4], F32)
    wth = sb("wth", [128, 4], F32); wmth = sb("wmth", [128, 4], F32)
    Hb = sb("Hb", [128, max(E, 4 * 128)], F32)
    ixfm = sb("ixfm", [128, KC * TOK], BF16)
    yacc = sb("yacc", [128, NB * D], F32)

    ARENA_BYTES = 98 * 1024
    arena = nc.alloc_sbuf_tensor("arena", [128, ARENA_BYTES], I8)
    ar_ap = {F32: arena.bitcast(F32).ap(), BF16: arena.bitcast(BF16).ap(),
             I8: arena.bitcast(I8).ap()}
    ELSZ = {F32: 4, BF16: 2, I8: 1, I32: 4}

    def trans_alloc():
        p = [0]
        def t(name, shape, dt):
            at = p[0]
            w = shape[1] * ELSZ[dt]
            p[0] = at + ((w + 31) // 32) * 32
            assert p[0] <= ARENA_BYTES, (name, p[0])
            return ar_ap[dt][:, at // ELSZ[dt]: at // ELSZ[dt] + shape[1]]
        return t

    t = trans_alloc()
    xa = [t(f"xa{i}", [128, D], F32) for i in range(2)]
    sqscr = t("sqscr", [128, max(D, WSF)], BF16)
    i8a = t("i8a", [128, D], I8)
    ibf = [t(f"ibf{i}", [128, D], BF16) for i in range(2)]
    ybufA = t("ybufA", [128, D], F32)
    rbufA = t("rbufA", [128, D], F32)
    cbufA = t("cbufA", [128, D], BF16)
    wtc_f = [t(f"wtcf{i}", [128, min(2048, 2 * E)], F32) for i in range(2)]
    wtc_b = t("wtcb", [128, min(2048, 2 * E)], BF16)
    ws_f = [t(f"wsf{i}", [128, 2048], F32) for i in range(2)]
    endA = None

    t = trans_alloc()
    _xg0 = t("xg0", [128, D], F32)
    xg = [_xg0, _xg0]
    xn_tm = [t(f"xntm{i}", [128, D], BF16) for i in range(2)]
    xn_fm = t("xnfm", [128, KC * GT], BF16)
    wts = [t(f"wts{i}", [128, KC * 512], BF16) for i in range(2)]
    qf = [t(f"qf{i}", [128, 512], F32) for i in range(2)]
    qb = [t(f"qb{i}", [128, 512], BF16) for i in range(2)]
    c1q = t("c1q", [128, 512], BF16)
    gsb = [t(f"gsb{i}", [128, GT], F32) for i in range(2)]
    qsb = [t(f"qsb{i}", [128, GT], F32) for i in range(2)]
    hscan = [t(f"hscan{i}", [128, GT], F32) for i in range(2)]
    pscan = [t(f"pscan{i}", [128, GT], F32) for i in range(2)]
    s_stg = [t(f"sstg{i}", [128, GT], F32) for i in range(2)]
    p_stg = [t(f"pstg{i}", [128, GT], F32) for i in range(2)]

    t = trans_alloc()
    wqg = [t(f"wqg{i}", [128, KC * 512], BF16) for i in range(2)]
    wqu = [t(f"wqu{i}", [128, KC * 512], BF16) for i in range(2)]
    sgate = [t(f"sgate{i}", [128, 512], F32) for i in range(2)]
    hst = [t(f"hst{i}", [128, 512], F32) for i in range(2)]

    t = trans_alloc()
    hfull = [t(f"hfull{i}", [128, H], F32) for i in range(2)]
    ihstage = t("ihstage", [128, HK * 128], BF16)
    i8h = t("i8h", [128, 512], I8)
    ibfh = [t(f"ibfh{i}", [128, 512], BF16) for i in range(2)]
    sqh_scr = t("sqh_scr", [128, H // 2], BF16)
    ybufH = t("ybufH", [128, 512], F32)
    rbufH = t("rbufH", [128, 512], F32)
    cbufH = t("cbufH", [128, 512], BF16)

    t = trans_alloc()
    wdq = [t(f"wdq{i}", [128, 8 * D], BF16) for i in range(2)]
    ihsb = [t(f"ihsb{i}", [128, 8 * 128], BF16) for i in range(2)]

    t = trans_alloc()
    s_tm = [t(f"stm{i}", [128, E], F32) for i in range(2)]
    p_tm = [t(f"ptm{i}", [128, E], F32) for i in range(2)]
    gfin = t("gfin", [128, D], F32)
    sgb = t("sgb", [128, D], F32)
    xod = [t(f"xod{i}", [128, D], F32) for i in range(2)]
    outsb = [t(f"outsb{i}", [128, D], F32) for i in range(2)]

    psA = nc.alloc_psum_tensor("psA", [128, 512], F32)
    psB = nc.alloc_psum_tensor("psB", [128, 512], F32)
    psC = nc.alloc_psum_tensor("psC", [128, 512], F32)
    psD = nc.alloc_psum_tensor("psD", [128, 512], F32)
    ptrA = nc.alloc_psum_tensor("ptrA", [128, 1024], BF16)
    ptrB = nc.alloc_psum_tensor("ptrB", [128, 1024], BF16)
    psS = nc.alloc_psum_tensor("psS", [128, 512], F32)
    psS2 = nc.alloc_psum_tensor("psS2", [128, 512], F32)

    sem_names = [
        "ph", "vch", "sch", "gch", "cml",
        "xa0", "xa1", "xbl0", "xbl1", "wtl0", "wtl1", "wtw",
        "wtsl0", "wtsl1", "wsl0", "wsl1", "hw0", "hw1", "qbw0", "qbw1",
        "hr0", "hr1", "ihw0", "ihw1", "ihr0", "ihr1", "wdl0", "wdl1", "c1l0", "c1l1",
        "sr0", "sr1", "xol0", "xol1", "outw0", "outw1", "sw0", "sw1", "ccs", "ccc", "ccd",
        "tpA", "tpB", "tpC", "tpD", "mmB", "mm1", "mm3",
    ]
    sems = {}
    from contextlib import ExitStack
    stack = ExitStack()
    for n in sem_names:
        sems[n] = stack.enter_context(nc.semaphore(n))

    WSTW = 2048
    wstat_tiles = []
    wranges = []
    for m, (ap_, R, C) in enumerate([(wgsh, SHD_W, H), (wush, SHD_W, H), (wdsh, SHD_D, D)]):
        lo = len(wstat_tiles)
        for r in range(max(1, R // 128)):
            for c0 in range(0, C, WSTW):
                wstat_tiles.append((ap_, r * 128, c0, min(WSTW, C - c0), min(128, R)))
        wranges.append((lo, len(wstat_tiles)))
    wquant_tiles = []
    for m, (sap, dap, R, C) in enumerate([(wgsh, wgqsh, SHD_W, H), (wush, wuqsh, SHD_W, H),
                                          (wdsh, wdqsh, SHD_D, D)]):
        for r in range(max(1, R // 128)):
            for c0 in range(0, C, 512):
                wquant_tiles.append((sap, dap, r * 128, c0, min(512, C - c0),
                                     min(128, R), m))

    CH = {"vector": "vch", "scalar": "sch", "gpsimd": "gch"}

    PHLIM = int(os.environ.get("KPHL", "99"))
    KSKIP = set(os.environ.get("KSKIP", "").split(","))

    def describe(real, eng, ctx):
        cnt = {n: 0 for n in sem_names}
        marks = {}
        phase_no = [0]

        def me(e):
            return e == real

        def opE(e, fn, mark=None):
            s = CH[e]
            if me(e):
                if cnt[s] > 0:
                    eng.wait_ge(sems[s], cnt[s])
                fn(eng).then_inc(sems[s], 1)
            cnt[s] += 1
            if mark is not None:
                marks[mark] = (s, cnt[s])

        def opT(fn, sem=None, mark=None):
            if me("tensor"):
                ins = fn(eng)
                if sem is not None:
                    ins.then_inc(sems[sem], 1)
            if sem is not None:
                cnt[sem] += 1
                if mark is not None:
                    marks[mark] = (sem, cnt[sem])

        def dma(e, dst, src, sem, mark=None):
            if me(e):
                eng.dma_start(dst, src).then_inc(sems[sem], 16)
            cnt[sem] += 16
            if mark is not None:
                marks[mark] = (sem, cnt[sem])

        def waitm(e, mark):
            s, v = marks[mark]
            if me(e) and v > 0:
                eng.wait_ge(sems[s], v)

        def waitv(e, sem, v):
            if me(e) and v > 0:
                eng.wait_ge(sems[sem], v)

        def waitcur(e, sem):
            waitv(e, sem, cnt[sem])

        GP_DMA = ["cml", "xa0", "xa1", "xbl0", "xbl1", "wtw", "hw0", "hw1",
                  "hr0", "hr1", "ihw0", "ihw1", "ihr0", "ihr1",
                  "xol0", "xol1", "outw0", "outw1", "sw0", "sw1"]
        GP_DMA += ["qbw0", "qbw1"]
        SY_DMA = ["wtl0", "wtl1", "wsl0", "wsl1", "c1l0", "c1l1",
                  "wdl0", "wdl1", "sr0", "sr1"]
        SC_DMA = ["wtsl0", "wtsl1"]
        ENG_DMA = {"gpsimd": GP_DMA, "sync": SY_DMA, "scalar": SC_DMA}

        def finish_early():
            for tb in range(NB):
                dma("gpsimd", out[tb * 128:(tb + 1) * 128, :], xod[0][:], f"outw{tb % 2}")
            waitv("gpsimd", "outw0", cnt["outw0"])
            waitv("gpsimd", "outw1", cnt["outw1"])

        def barrier():
            cnt["ph"] += 5
            if real in ENGS:
                drains = ENG_DMA.get(real, [])
                for s_ in drains:
                    if cnt[s_] > 0:
                        eng.wait_ge(sems[s_], cnt[s_])
                if real in CH and cnt[CH[real]] > 0:
                    eng.wait_ge(sems[CH[real]], cnt[CH[real]])
                eng.sem_inc(sems["ph"], 1)
                eng.wait_ge(sems["ph"], cnt["ph"])
            phase_no[0] += 1
            if phase_no[0] >= PHLIM:
                finish_early()
                return True
            return False

        tstates = {}

        def trstream(tag, tpsem, pairs, ident_ap, banks, gsz, src_mark, ev="vector"):
            st = tstates.setdefault(tag, {"grp": 0, "bmark": {0: None, 1: None}})
            last_ev = None
            i = 0
            while i < len(pairs):
                chunk = pairs[i:i + gsz]
                b = st["grp"] % 2
                bank = banks[b]
                if st["bmark"][b] is not None:
                    waitm("tensor", st["bmark"][b])
                if src_mark is not None:
                    waitm("tensor", src_mark)
                for sl, (srcap, dstap) in enumerate(chunk):
                    opT(lambda e, srcap=srcap, bank=bank, sl=sl, ident_ap=ident_ap:
                        e.transpose(bank[:, sl * 128:(sl + 1) * 128], srcap, ident_ap),
                        tpsem, mark=(tag, "tp", st["grp"]) if sl == len(chunk) - 1 else None)
                waitm(ev, (tag, "tp", st["grp"]))
                for sl, (srcap, dstap) in enumerate(chunk):
                    if ev == "scalar":
                        opE(ev, lambda e, dstap=dstap, bank=bank, sl=sl:
                            e.activation(dstap, bank[:, sl * 128:(sl + 1) * 128], AF.Copy),
                            mark=(tag, "ev", st["grp"]) if sl == len(chunk) - 1 else None)
                    else:
                        opE(ev, lambda e, dstap=dstap, bank=bank, sl=sl:
                            e.tensor_copy(dstap, bank[:, sl * 128:(sl + 1) * 128]),
                            mark=(tag, "ev", st["grp"]) if sl == len(chunk) - 1 else None)
                st["bmark"][b] = (tag, "ev", st["grp"])
                last_ev = (tag, "ev", st["grp"])
                st["grp"] += 1
                i += gsz
            return marks[last_ev]

        opE("gpsimd", lambda e: e.iota(identi[:], [[1, 128]], base=0, channel_multiplier=-1))
        opE("gpsimd", lambda e: e.tensor_scalar(ident[:], identi[:], 0, None, ALU.is_equal))
        opE("gpsimd", lambda e: e.tensor_scalar(identf[:], identi[:], 0, None, ALU.is_equal))
        opE("gpsimd", lambda e: e.memset(ones_col[:], 1.0))
        opE("gpsimd", lambda e: e.memset(ones_row[:], 1.0))
        opE("gpsimd", lambda e: e.memset(carry[:], 0.0))
        opE("gpsimd", lambda e: e.memset(carryP[:], 1.0))
        opE("gpsimd", lambda e: e.memset(Hcar[:], 0.0))
        opE("gpsimd", lambda e: e.memset(cstg[:], 0.0))
        opE("gpsimd", lambda e: e.memset(wacc[:], 0.0))
        opE("gpsimd", lambda e: e.memset(c5[:], 1e-5))
        opE("gpsimd", lambda e: e.memset(c6[:], 1e-6))
        dma("gpsimd", cm_sb[:], cm, "cml")
        waitv("gpsimd", "cml", 16)
        if barrier():
            return

        for tb in range(NB):
            p = tb % 2
            if tb >= 2:
                waitm("gpsimd", ("Ava", tb - 2))
                waitm("gpsimd", ("Asa", tb - 2))
            dma("gpsimd", xa[p][:], xo[tb * 128:(tb + 1) * 128, :], f"xa{p}", mark=("Axa", tb))
            waitm("vector", ("Axa", tb))
            opE("vector", lambda e, p=p, tb=tb: e.reduce_sum(musum[:, tb:tb + 1], xa[p][:], AX.X))
            opE("vector", lambda e, p=p, tb=tb: e.reduce_max(maxx[:, tb:tb + 1], xa[p][:], AX.X))
            opE("vector", lambda e, p=p, tb=tb: e.tensor_reduce(
                minn[:, tb:tb + 1], xa[p][:], AX.X, ALU.min), mark=("Ava", tb))
            waitm("scalar", ("Axa", tb))
            opE("scalar", lambda e, p=p, tb=tb: e.activation(
                sqscr[:, :D], xa[p][:], AF.Square, accum_out=sumsq[:, tb:tb + 1]),
                mark=("Asa", tb))
        waitm("vector", ("Ava", NB - 1))
        waitm("vector", ("Asa", NB - 1))
        nbs = slice(0, NB)
        opE("vector", lambda e: e.tensor_scalar(mu[:, nbs], musum[:, nbs], 1.0 / D, None, ALU.mult))
        opE("vector", lambda e: e.tensor_scalar(tmpa[:, nbs], sumsq[:, nbs], 1.0 / D, None, ALU.mult))
        opE("vector", lambda e: e.tensor_tensor(tmpb[:, nbs], mu[:, nbs], mu[:, nbs], ALU.mult))
        opE("vector", lambda e: e.tensor_tensor(tmpa[:, nbs], tmpa[:, nbs], tmpb[:, nbs], ALU.subtract),
            mark=("Avar",))
        waitm("scalar", ("Avar",))
        opE("scalar", lambda e: e.activation(tmpb[:, nbs], tmpa[:, nbs], AF.Sqrt, bias=c5[:]),
            mark=("Asq1",))
        waitm("vector", ("Asq1",))
        opE("vector", lambda e: e.reciprocal(rho[:, nbs], tmpb[:, nbs]))
        opE("vector", lambda e: e.tensor_scalar(tmpb[:, nbs], tmpa[:, nbs], 1e-5, None, ALU.add))
        opE("vector", lambda e: e.reciprocal(tmpc[:, nbs], tmpb[:, nbs]))
        opE("vector", lambda e: e.tensor_tensor(tmpb[:, nbs], tmpa[:, nbs], tmpc[:, nbs], ALU.mult),
            mark=("Aq2",))
        waitm("scalar", ("Aq2",))
        opE("scalar", lambda e: e.activation(tmpc[:, nbs], tmpb[:, nbs], AF.Sqrt, bias=c6[:]),
            mark=("Asq2",))
        waitm("vector", ("Asq2",))
        opE("vector", lambda e: e.reciprocal(rr[:, nbs], tmpc[:, nbs]))
        opE("vector", lambda e: e.tensor_tensor(tmpa[:, nbs], maxx[:, nbs], mu[:, nbs], ALU.subtract))
        opE("vector", lambda e: e.tensor_tensor(tmpb[:, nbs], mu[:, nbs], minn[:, nbs], ALU.subtract))
        opE("vector", lambda e: e.tensor_tensor(tmpa[:, nbs], tmpa[:, nbs], tmpb[:, nbs], ALU.max))
        opE("vector", lambda e: e.tensor_tensor(tmpa[:, nbs], tmpa[:, nbs], rho[:, nbs], ALU.mult))
        opE("vector", lambda e: e.tensor_tensor(tmpa[:, nbs], tmpa[:, nbs], rr[:, nbs], ALU.mult))
        opE("vector", lambda e: e.tensor_scalar(tmpa[:, nbs], tmpa[:, nbs], 1e-5, None, ALU.max))
        opE("vector", lambda e: e.tensor_scalar(sinv[:, nbs], tmpa[:, nbs], 1.0 / 127.0, None, ALU.mult))
        opE("vector", lambda e: e.reciprocal(tmpb[:, nbs], tmpa[:, nbs]))
        opE("vector", lambda e: e.tensor_tensor(tmpc[:, nbs], rho[:, nbs], rr[:, nbs], ALU.mult))
        opE("vector", lambda e: e.tensor_tensor(tmpc[:, nbs], tmpc[:, nbs], tmpb[:, nbs], ALU.mult))
        opE("vector", lambda e: e.tensor_scalar(alpha[:, nbs], tmpc[:, nbs], 127.0, None, ALU.mult),
            mark=("Astat",))

        wtcW = min(2048, 2 * E)
        wtcR = min(128, SHD_W)
        wtc_tiles = [(k, c0) for k in range(max(1, SHD_W // 128))
                     for c0 in range(0, 2 * E, wtcW)]
        wtc_per = (len(wtc_tiles) + NB - 1) // NB
        ws_per = (len(wstat_tiles) + NB - 1) // NB
        for tb in range(NB):
            p = tb % 2
            waitm("gpsimd", ("Astat",))
            if tb >= 2:
                waitm("gpsimd", ("Aq", tb - 2))
            dma("gpsimd", xa[p][:], xo[tb * 128:(tb + 1) * 128, :], f"xa{p}", mark=("Axa2", tb))
            waitm("vector", ("Axa2", tb))
            opE("vector", lambda e, p=p, tb=tb: e.tensor_scalar(
                ybufA[:], xa[p][:], mu[:, tb:tb + 1], alpha[:, tb:tb + 1], ALU.subtract, ALU.mult))
            opE("vector", lambda e: e.tensor_scalar(i8a[:], ybufA[:], 1.0, None, ALU.mult))
            opE("vector", lambda e: e.tensor_tensor(rbufA[:], ybufA[:], i8a[:], ALU.subtract))
            opE("vector", lambda e: e.tensor_scalar(cbufA[:], rbufA[:], 0.5, None, ALU.is_gt))
            opE("vector", lambda e: e.scalar_tensor_tensor(
                ybufA[:], rbufA[:], -0.5, cbufA[:], ALU.is_lt, ALU.subtract))
            if tb >= 2:
                waitm("vector", ("AevA", tb - 2))
            opE("vector", lambda e, p=p: e.tensor_tensor(
                ibf[p][:], i8a[:], ybufA[:], ALU.subtract), mark=("Aq", tb))
            pairsA = [(ibf[p][:, c * 128:(c + 1) * 128],
                       ixfm[:, c * TOK + tb * 128: c * TOK + (tb + 1) * 128])
                      for c in range(KC)]
            marks[("AevA", tb)] = trstream(
                "A", "tpA", pairsA, ident[:], [ptrA, ptrB], 8, ("Aq", tb))
            for i in range(tb * wtc_per, min((tb + 1) * wtc_per, len(wtc_tiles))):
                k, c0 = wtc_tiles[i]
                pw = i % 2
                if i >= 2:
                    waitm("sync", ("Awtv", i - 2))
                dma("sync", wtc_f[pw][:wtcR, :wtcW],
                    wtsh[k * 128:k * 128 + wtcR, c0:c0 + wtcW],
                    f"wtl{pw}", mark=("Awtl", i))
                waitm("vector", ("Awtl", i))
                if i >= 1:
                    waitm("vector", ("Awtw", i - 1))
                opE("vector", lambda e, pw=pw: e.tensor_copy(
                    wtc_b[:wtcR, :wtcW], wtc_f[pw][:wtcR, :wtcW]), mark=("Awtv", i))
                waitm("gpsimd", ("Awtv", i))
                dma("gpsimd", wtbfsh[k * 128:k * 128 + wtcR, c0:c0 + wtcW],
                    wtc_b[:wtcR, :wtcW], "wtw", mark=("Awtw", i))
            for i in range(tb * ws_per, min((tb + 1) * ws_per, len(wstat_tiles))):
                ap_, r0, c0, fd, nr = wstat_tiles[i]
                pw = i % 2
                if i >= 2:
                    waitm("sync", ("Wab", i - 2))
                dma("sync", ws_f[pw][:nr, :fd], ap_[r0:r0 + nr, c0:c0 + fd],
                    f"wsl{pw}", mark=("Wsl", i))
                waitm("scalar", ("Wsl", i))
                opE("scalar", lambda e, pw=pw, fd=fd, nr=nr, i=i: e.activation(
                    sqscr[:nr, :fd], ws_f[pw][:nr, :fd], AF.Abs, accum_out=wacc[:nr, i:i + 1]),
                    mark=("Wab", i))
        if barrier():
            return

        waitv("gpsimd", "wtw", cnt["wtw"])
        if me("gpsimd"):
            eng.collective_compute(
                "AllGather", ALU.bypass, replica_groups=[list(range(8))],
                ins=[wtbfsh_d.ap().opt()], outs=[wtbf_d.ap().opt()]).then_inc(sems["ccs"], 1)
        cnt["ccs"] += 1
        marks[("AGwt",)] = ("ccs", cnt["ccs"])
        opE("gpsimd", lambda e: e.memset(cstg[:], 0.0), mark=("Cst0",))
        for m in range(3):
            lo, hi = wranges[m]
            waitm("vector", ("Wab", hi - 1))
            if m == 0:
                waitm("vector", ("Cst0",))
            opE("vector", lambda e, m=m, lo=lo, hi=hi: e.reduce_sum(
                cstg[:, m:m + 1], wacc[:, lo:hi], AX.X), mark=("Wsum", m))
        waitm("gpsimd", ("Wsum", 2))
        dma("gpsimd", ccin_d.ap(), cstg[:], "ccd")
        waitv("gpsimd", "ccd", cnt["ccd"])
        if me("gpsimd"):
            eng.collective_compute(
                "AllGather", ALU.bypass, replica_groups=[list(range(8))],
                ins=[ccin_d.ap().opt()], outs=[ccout_d.ap().opt()]).then_inc(sems["ccs"], 1)
        cnt["ccs"] += 1
        waitv("gpsimd", "ccs", cnt["ccs"])
        for r in range(8):
            dma("gpsimd", ccg[:, r * 32:(r + 1) * 32],
                ccout_d.ap()[r * 128:(r + 1) * 128, :], "ccd",
                mark=("WSg",) if r == 7 else None)
        waitm("vector", ("WSg",))
        opE("vector", lambda e: e.tensor_copy(wbar[:, 0:3], ccg[:, 0:3]))
        for r in range(1, 8):
            opE("vector", lambda e, r=r: e.tensor_tensor(
                wbar[:, 0:3], wbar[:, 0:3], ccg[:, r * 32: r * 32 + 3], ALU.add),
                mark=("WSsum",) if r == 7 else None)
        waitm("tensor", ("WSsum",))
        for m in range(3):
            opT(lambda e, m=m: e.matmul(
                psS[0:1, m:m + 1], ones_col[:], wbar[:, m:m + 1], start=True, stop=True),
                "mm1", mark=("Wmm1", m))
        waitm("vector", ("Wmm1", 2))
        for m in range(3):
            opE("vector", lambda e, m=m: e.tensor_copy(wclip[0:1, m:m + 1], psS[0:1, m:m + 1]),
                mark=("Wcp", m))
        waitm("tensor", ("Wcp", 2))
        for m in range(3):
            opT(lambda e, m=m: e.matmul(
                psS[:, 8 + m:9 + m], ones_row[:], wclip[0:1, m:m + 1], start=True, stop=True),
                "mm1", mark=("Wmm2", m))
        waitm("vector", ("Wmm2", 2))
        opE("vector", lambda e: e.tensor_copy(wbar[:, 0:3], psS[:, 8:11]))
        opE("vector", lambda e: e.tensor_scalar(wbar[:, 0:3], wbar[:, 0:3], 1.0 / (D * H), None, ALU.mult))
        opE("vector", lambda e: e.tensor_scalar(wclip[:, 0:3], wbar[:, 0:3], 1e-5, None, ALU.max))
        opE("vector", lambda e: e.tensor_scalar(wth[:, 0:3], wclip[:, 0:3], 0.5, None, ALU.mult))
        opE("vector", lambda e: e.tensor_scalar(wmth[:, 0:3], wclip[:, 0:3], -0.5, None, ALU.mult))
        opE("vector", lambda e: e.tensor_scalar(scg[:, nbs], sinv[:, nbs], wclip[:, 0:1], -1.0, ALU.mult, ALU.mult))
        opE("vector", lambda e: e.tensor_scalar(scu[:, nbs], sinv[:, nbs], wclip[:, 1:2], -1.0, ALU.mult, ALU.mult),
            mark=("Wready",))

        qt_slots = max(1, (NG * NPG * 2) // 5)
        qt_per = (len(wquant_tiles) + qt_slots - 1) // qt_slots
        qt_next = [0]
        N_G = sum(1 for tle in wquant_tiles if tle[6] < 1)
        N_GU = sum(1 for tle in wquant_tiles if tle[6] < 2)
        NPALL = NG * NPG
        P_G = NPALL // 2
        P_U = (NPALL * 2) // 3

        def fire_ags(pairs, mk):
            waitv("gpsimd", "qbw0", cnt["qbw0"])
            waitv("gpsimd", "qbw1", cnt["qbw1"])
            for (insh, outfull) in pairs:
                if me("gpsimd"):
                    eng.collective_compute(
                        "AllGather", ALU.bypass, replica_groups=[list(range(8))],
                        ins=[insh.ap().opt()], outs=[outfull.ap().opt()]).then_inc(sems["ccs"], 1)
                cnt["ccs"] += 1
            marks[mk] = ("ccs", cnt["ccs"])

        def maybe_fire(p):
            if qt_next[0] >= N_G and p >= P_G and ("AGg",) not in marks:
                fire_ags([(wgqsh_d, wgq_d)], ("AGg",))
            if qt_next[0] >= N_GU and p >= P_U and ("AGu",) not in marks:
                fire_ags([(wuqsh_d, wuq_d)], ("AGu",))

        def quant_slot():
            hi = min(qt_next[0] + qt_per, len(wquant_tiles))
            for i in range(qt_next[0], hi):
                sap, dap, r0, c0, w, nr, m = wquant_tiles[i]
                pw = i % 2
                if i >= 2:
                    waitm("sync", ("Qq", i - 2))
                dma("sync", qf[pw][:nr, :w], sap[r0:r0 + nr, c0:c0 + w],
                    f"wsl{pw}", mark=("Ql", i))
                waitm("vector", ("Ql", i))
                if i >= 2:
                    waitm("vector", ("Qw", i - 2))
                opE("vector", lambda e, pw=pw, nr=nr, w=w, m=m: e.tensor_scalar(
                    c1q[:nr, :w], qf[pw][:nr, :w], wth[:nr, m:m + 1], None, ALU.is_gt))
                opE("vector", lambda e, pw=pw, nr=nr, w=w, m=m: e.scalar_tensor_tensor(
                    qb[pw][:nr, :w], qf[pw][:nr, :w], wmth[:nr, m:m + 1], c1q[:nr, :w],
                    ALU.is_lt, ALU.subtract), mark=("Qq", i))
                waitm("gpsimd", ("Qq", i))
                dma("gpsimd", dap[r0:r0 + nr, c0:c0 + w], qb[pw][:nr, :w], f"qbw{pw}",
                    mark=("Qw", i))
            qt_next[0] = hi

        waitm("scalar", ("AGwt",))
        waitm("sync", ("AGwt",))
        for g in range(NG):
            for tbp in range(GB):
                p = tbp % 2
                bidx = g * GB + tbp
                r0 = g * GT + tbp * 128
                prev_same_par = [t2 for t2 in range(GB) if t2 % 2 == tbp % 2][-1]
                if tbp >= 1:
                    waitm("gpsimd", ("Bxn", g, tbp - 1))
                elif g > 0:
                    waitm("gpsimd", ("Bxn", g - 1, GB - 1))
                dma("gpsimd", xg[p][:], xo[r0:r0 + 128, :], f"xbl{p}", mark=("Bxb2", g, tbp))
                waitm("vector", ("Bxb2", g, tbp))
                if g > 0 or tbp >= 2:
                    pm = (("BevX", g, tbp - 2) if tbp >= 2
                          else ("BevX", g - 1, prev_same_par))
                    waitm("vector", pm)
                opE("vector", lambda e, p=p, bidx=bidx: e.tensor_scalar(
                    xn_tm[p][:], xg[p][:], mu[:, bidx:bidx + 1], rho[:, bidx:bidx + 1],
                    ALU.subtract, ALU.mult), mark=("Bxn", g, tbp))
                if g > 0 and tbp == 0:
                    waitm("vector", ("BmmAll", g - 1))
                pairsB = [(xn_tm[p][:, c * 128:(c + 1) * 128],
                           xn_fm[:, c * GT + tbp * 128: c * GT + (tbp + 1) * 128])
                          for c in range(KC)]
                marks[("BevX", g, tbp)] = trstream(
                    "B", "tpB", pairsB, ident[:], [ptrA, ptrB], 8, ("Bxn", g, tbp))
            marks[("BevG", g)] = marks[("BevX", g, GB - 1)]
            for pg in range(NPG):
                sp = pg % 2
                qeng = "sync" if (g * NPG + pg) >= max(0, NPALL - 8) else "scalar"
                if pg >= 2 or g > 0:
                    pm = (("BmmPG", g, pg - 2) if pg >= 2
                          else ("BmmPG", g - 1, (pg - 2) % NPG))
                    waitm(qeng, pm)
                for k in range(KC):
                    dma(qeng, wts[sp][:, k * 512:(k + 1) * 512],
                        wtbf[k * 128:(k + 1) * 128, pg * 512:(pg + 1) * 512], f"wtsl{sp}")
                marks[("Bwtsl", g, pg)] = (f"wtsl{sp}", cnt[f"wtsl{sp}"])
                for pi in range(2):
                    cc = pg * 2 + pi
                    cp = cc % 2
                    pq = psA if cp == 0 else psC
                    pgm = psB if cp == 0 else psD
                    waitm("tensor", ("Bwtsl", g, pg))
                    waitm("tensor", ("BevG", g))
                    gidx = g * CC + cc
                    if gidx >= 2:
                        waitm("tensor", ("Bqv", gidx - 2))
                        waitm("tensor", ("Bsg", gidx - 2))
                    for k in range(KC):
                        opT(lambda e, k=k, sp=sp, pi=pi, pq=pq: e.matmul(
                            pq[:, :GT], wts[sp][:, k * 512 + pi * 256: k * 512 + pi * 256 + 128],
                            xn_fm[:, k * GT:(k + 1) * GT],
                            start=(k == 0), stop=(k == KC - 1)))
                        opT(lambda e, k=k, sp=sp, pi=pi, pgm=pgm: e.matmul(
                            pgm[:, :GT], wts[sp][:, k * 512 + pi * 256 + 128: k * 512 + pi * 256 + 256],
                            xn_fm[:, k * GT:(k + 1) * GT],
                            start=(k == 0), stop=(k == KC - 1)),
                            "mmB" if k == KC - 1 else None,
                            mark=("BmmP", gidx) if k == KC - 1 else None)
                    waitm("scalar", ("BmmP", gidx))
                    if gidx >= 2:
                        waitm("scalar", ("Bscn", gidx - 2))
                    opE("scalar", lambda e, cp=cp, pgm=pgm: e.activation(
                        gsb[cp][:], pgm[:, :GT], AF.Sigmoid), mark=("Bsg", gidx))
                    waitm("vector", ("BmmP", gidx))
                    if gidx >= 2:
                        waitm("vector", ("Bscn", gidx - 2))
                    opE("vector", lambda e, cp=cp, pq=pq: e.tensor_scalar(
                        qsb[cp][:], pq[:, :GT], 1e-20, None, ALU.max), mark=("Bqv", gidx))
                    waitm("vector", ("Bsg", gidx))
                    if gidx >= 2:
                        waitm("vector", ("BSev", gidx - 2))
                    opE("vector", lambda e, cp=cp, cc=cc: e.tensor_tensor_scan(
                        hscan[cp][:], gsb[cp][:], qsb[cp][:], carry[:, cc:cc + 1],
                        ALU.mult, ALU.add))
                    opE("vector", lambda e, cp=cp, cc=cc: e.tensor_copy(
                        carry[:, cc:cc + 1], hscan[cp][:, GT - 1:GT]))
                    opE("vector", lambda e, cp=cp, cc=cc: e.tensor_tensor_scan(
                        pscan[cp][:], gsb[cp][:], gsb[cp][:], carryP[:, cc:cc + 1],
                        ALU.mult, ALU.bypass))
                    opE("vector", lambda e, cp=cp, cc=cc: e.tensor_copy(
                        carryP[:, cc:cc + 1], pscan[cp][:, GT - 1:GT]), mark=("Bscn", gidx))
                    if gidx >= 2:
                        waitm("vector", ("Bsw", gidx - 2))
                    trstream("BS", "tpD",
                             [(hscan[cp][:, ti * 128:(ti + 1) * 128],
                               s_stg[cp][:, ti * 128:(ti + 1) * 128]) for ti in range(GB)],
                             identf[:], [psS, psS2], 4, ("Bscn", gidx))
                    marks[("BSev", gidx)] = trstream(
                        "BS", "tpD",
                        [(pscan[cp][:, ti * 128:(ti + 1) * 128],
                          p_stg[cp][:, ti * 128:(ti + 1) * 128]) for ti in range(GB)],
                        identf[:], [psS, psS2], 4, ("Bscn", gidx))
                    waitm("gpsimd", ("BSev", gidx))
                    for ti in range(GB):
                        dma("gpsimd",
                            s_a[g * GT + ti * 128: g * GT + (ti + 1) * 128,
                                cc * 128:(cc + 1) * 128],
                            s_stg[cp][:, ti * 128:(ti + 1) * 128], f"sw{cp}")
                        dma("gpsimd",
                            p_a[g * GT + ti * 128: g * GT + (ti + 1) * 128,
                                cc * 128:(cc + 1) * 128],
                            p_stg[cp][:, ti * 128:(ti + 1) * 128],
                            f"sw{cp}", mark=("Bsw", gidx) if ti == GB - 1 else None)
                marks[("BmmPG", g, pg)] = ("mmB", cnt["mmB"])
                quant_slot()
                maybe_fire(g * NPG + pg + 1)
            marks[("BmmAll", g)] = ("mmB", cnt["mmB"])
        quant_slot()
        maybe_fire(NPALL * 2)
        opE("vector", lambda e: e.tensor_copy(cstg[:, 0:CC], carry[:, 0:CC]))
        opE("vector", lambda e: e.tensor_copy(cstg[:, 16:16 + CC], carryP[:, 0:CC]),
            mark=("CCstg",))
        waitm("gpsimd", ("CCstg",))
        dma("gpsimd", ccin_d.ap(), cstg[:], "ccd")
        waitv("gpsimd", "ccd", cnt["ccd"])
        if me("gpsimd"):
            eng.collective_compute(
                "AllGather", ALU.bypass, replica_groups=[list(range(8))],
                ins=[ccin_d.ap().opt()], outs=[ccout_d.ap().opt()]).then_inc(sems["ccc"], 1)
        cnt["ccc"] += 1
        marks[("AGc",)] = ("ccc", cnt["ccc"])
        fire_ags([(wdqsh_d, wdq_d)], ("AGw",))
        if barrier():
            return

        waitm("gpsimd", ("AGc",))
        for r in range(8):
            dma("gpsimd", ccg[:, r * 32:(r + 1) * 32],
                ccout_d.ap()[r * 128:(r + 1) * 128, :], "ccd",
                mark=("CCg",) if r == 7 else None)
        waitm("vector", ("CCg",))
        for r in range(8):
            opE("vector", lambda e, r=r: e.tensor_scalar(
                tmpH[:, 0:CC], ccg[:, r * 32: r * 32 + CC], cm_sb[:, r:r + 1], None, ALU.mult))
            opE("vector", lambda e, r=r: e.tensor_scalar(
                tmpP[:, 0:CC], ccg[:, r * 32 + 16: r * 32 + 16 + CC],
                cm_sb[:, r:r + 1], cm_sb[:, 8 + r:9 + r], ALU.mult, ALU.add))
            opE("vector", lambda e: e.tensor_tensor(tmpQ[:, 0:CC], tmpP[:, 0:CC], Hcar[:, 0:CC], ALU.mult))
            opE("vector", lambda e: e.tensor_tensor(Hcar[:, 0:CC], tmpQ[:, 0:CC], tmpH[:, 0:CC], ALU.add),
                mark=("CCH",) if r == 7 else None)
        waitm("tensor", ("CCH",))
        opT(lambda e: e.transpose(psS[0:CC, 0:128], Hcar[:, 0:CC], identf[:]),
            "tpD", mark=("CCHt",))
        waitm("vector", ("CCHt",))
        opE("vector", lambda e: e.tensor_copy(Htr[0:CC, :], psS[0:CC, 0:128]), mark=("CCHe",))
        for ccx in range(CC):
            if ccx == 0:
                waitm("gpsimd", ("CCHe",))
            else:
                waitm("gpsimd", ("DHm", ccx - 1))
            dma("gpsimd", Rrow[0:1, :], Htr[ccx:ccx + 1, :], "ccd", mark=("DHr", ccx))
            waitm("tensor", ("DHr", ccx))
            bank = [psS, psS2][ccx % 2]
            if ccx >= 2:
                waitm("tensor", ("DHe", ccx - 2))
            opT(lambda e, bank=bank: e.matmul(
                bank[:, 0:128], ones_row[:], Rrow[0:1, :], start=True, stop=True),
                "mmB", mark=("DHm", ccx))
            waitm("vector", ("DHm", ccx))
            opE("vector", lambda e, ccx=ccx, bank=bank: e.tensor_copy(
                Hb[:, ccx * 128:(ccx + 1) * 128], bank[:, 0:128]),
                mark=("DHe", ccx))
        marks[("DHb",)] = marks[("DHe", CC - 1)]

        waitm("sync", ("AGu",))
        for n in range(NSL):
            qp = n % 2
            for (mat, wap, wqbs) in [("g", wgq, wqg), ("u", wuq, wqu)]:
                if n >= 2:
                    waitm("sync", ("C1mmN", n - 2))
                for k in range(KC):
                    dma("sync", wqbs[qp][:, k * 512:(k + 1) * 512],
                        wap[k * 128:(k + 1) * 128, n * 512:(n + 1) * 512], f"c1l{qp}")
                marks[("C1q", n, mat)] = (f"c1l{qp}", cnt[f"c1l{qp}"])
            for tb in range(NB):
                idx = n * NB + tb
                tp = idx % 2
                pg_ = psA if tp == 0 else psC
                pu_ = psB if tp == 0 else psD
                waitm("tensor", ("C1q", n, "u"))
                if idx >= 2:
                    waitm("tensor", ("C1sil", idx - 2))
                    waitm("tensor", ("C1hsv", idx - 2))
                for k in range(KC):
                    opT(lambda e, k=k, tb=tb, pg_=pg_, qp=qp: e.matmul(
                        pg_[:], ixfm[:, k * TOK + tb * 128: k * TOK + (tb + 1) * 128],
                        wqg[qp][:, k * 512:(k + 1) * 512], start=(k == 0), stop=(k == KC - 1)))
                    opT(lambda e, k=k, tb=tb, pu_=pu_, qp=qp: e.matmul(
                        pu_[:], ixfm[:, k * TOK + tb * 128: k * TOK + (tb + 1) * 128],
                        wqu[qp][:, k * 512:(k + 1) * 512], start=(k == 0), stop=(k == KC - 1)),
                        "mm1" if k == KC - 1 else None,
                        mark=("C1mmP", idx) if k == KC - 1 else None)
                if tb == NB - 1:
                    marks[("C1mmN", n)] = ("mm1", cnt["mm1"])
                waitm("scalar", ("C1mmP", idx))
                if idx >= 2:
                    waitm("scalar", ("C1hsv", idx - 2))
                opE("scalar", lambda e, tp=tp, tb=tb, pg_=pg_: e.activation(
                    sgate[tp][:], pg_[:], AF.Silu, scale=scg[:, tb:tb + 1]), mark=("C1sil", idx))
                waitm("vector", ("C1mmP", idx))
                waitm("vector", ("C1sil", idx))
                if idx >= 2:
                    waitm("vector", ("C1hw", idx - 2))
                opE("vector", lambda e, tp=tp, tb=tb, pu_=pu_: e.scalar_tensor_tensor(
                    hst[tp][:], pu_[:], scu[:, tb:tb + 1], sgate[tp][:], ALU.mult, ALU.mult),
                    mark=("C1hsv", idx))
                waitm("gpsimd", ("C1hsv", idx))
                dma("gpsimd", h_a[tb * 128:(tb + 1) * 128, n * 512:(n + 1) * 512],
                    hst[tp][:], f"hw{tp}", mark=("C1hw", idx))
        if barrier():
            return

        waitv("gpsimd", "hw0", cnt["hw0"])
        waitv("gpsimd", "hw1", cnt["hw1"])
        for tb in range(NB):
            hp = tb % 2
            if tb >= 2:
                waitm("gpsimd", ("C2strips", tb - 2))
            dma("gpsimd", hfull[hp][:], h_a[tb * 128:(tb + 1) * 128, :], f"hr{hp}", mark=("C2hr", tb))
            waitm("scalar", ("C2hr", tb))
            opE("scalar", lambda e, hp=hp, tb=tb: e.activation(
                sqh_scr[:], hfull[hp][:, :H // 2], AF.Square, accum_out=hsq1[:, tb:tb + 1]))
            opE("scalar", lambda e, hp=hp, tb=tb: e.activation(
                sqh_scr[:], hfull[hp][:, H // 2:], AF.Square, accum_out=hsq2[:, tb:tb + 1]),
                mark=("C2sq", tb))
            waitm("vector", ("C2hr", tb))
            opE("vector", lambda e, hp=hp, tb=tb: e.reduce_max(
                maxh[:, tb:tb + 1], hfull[hp][:], AX.X, apply_absolute_value=True))
            ts_ = slice(tb, tb + 1)
            waitm("vector", ("C2sq", tb))
            opE("vector", lambda e, ts_=ts_: e.tensor_tensor(
                sumsqh[:, ts_], hsq1[:, ts_], hsq2[:, ts_], ALU.add))
            opE("vector", lambda e, ts_=ts_: e.tensor_scalar(
                tmpa[:, ts_], sumsqh[:, ts_], 1.0 / H, None, ALU.mult), mark=("C2e", tb))
            waitm("scalar", ("C2e", tb))
            opE("scalar", lambda e, ts_=ts_: e.activation(
                tmpb[:, ts_], tmpa[:, ts_], AF.Sqrt, bias=c6[:]), mark=("C2sqh", tb))
            waitm("vector", ("C2sqh", tb))
            opE("vector", lambda e, ts_=ts_: e.reciprocal(tmpc[:, ts_], tmpb[:, ts_]))
            opE("vector", lambda e, ts_=ts_: e.tensor_tensor(tmpa[:, ts_], maxh[:, ts_], tmpc[:, ts_], ALU.mult))
            opE("vector", lambda e, ts_=ts_: e.tensor_scalar(tmpa[:, ts_], tmpa[:, ts_], 1e-5, None, ALU.max))
            opE("vector", lambda e, ts_=ts_: e.tensor_scalar(sinvh[:, ts_], tmpa[:, ts_], 1.0 / 127.0, None, ALU.mult))
            opE("vector", lambda e, ts_=ts_: e.reciprocal(tmpb[:, ts_], tmpa[:, ts_]))
            opE("vector", lambda e, ts_=ts_: e.tensor_tensor(tmpc[:, ts_], tmpc[:, ts_], tmpb[:, ts_], ALU.mult))
            opE("vector", lambda e, ts_=ts_: e.tensor_scalar(alphah[:, ts_], tmpc[:, ts_], 127.0, None, ALU.mult),
                mark=("C2a", tb))
            if tb >= 1 and "c2ihw" not in KSKIP:
                waitm("scalar", ("C2ihw", tb - 1))
            for st in (range(H // 512) if "c2st" not in KSKIP else []):
                bp = st % 2
                opE("vector", lambda e, hp=hp, st=st, tb=tb: e.tensor_scalar(
                    ybufH[:], hfull[hp][:, st * 512:(st + 1) * 512], alphah[:, tb:tb + 1], None, ALU.mult))
                opE("vector", lambda e: e.tensor_scalar(i8h[:], ybufH[:], 1.0, None, ALU.mult))
                opE("vector", lambda e: e.tensor_tensor(rbufH[:], ybufH[:], i8h[:], ALU.subtract))
                opE("vector", lambda e: e.tensor_scalar(cbufH[:], rbufH[:], 0.5, None, ALU.is_gt))
                opE("vector", lambda e: e.scalar_tensor_tensor(
                    ybufH[:], rbufH[:], -0.5, cbufH[:], ALU.is_lt, ALU.subtract))
                if st >= 2 and "c2tr" not in KSKIP:
                    waitm("vector", ("C2ev", tb, st - 2))
                opE("vector", lambda e, bp=bp: e.tensor_tensor(
                    ibfh[bp][:], i8h[:], ybufH[:], ALU.subtract),
                    mark=("C2ib", tb, st))
                if "c2tr" not in KSKIP:
                    pairsC = [(ibfh[bp][:, c4 * 128:(c4 + 1) * 128],
                               ihstage[:, (st * 4 + c4) * 128:(st * 4 + c4 + 1) * 128])
                              for c4 in range(4)]
                    marks[("C2ev", tb, st)] = trstream(
                        "C", "tpC", pairsC, ident[:], [ptrA, ptrB], 8, ("C2ib", tb, st),
                        ev="scalar")
            marks[("C2strips", tb)] = ("vch", cnt["vch"])
            if "c2st" not in KSKIP and "c2tr" not in KSKIP:
                marks[("C2evall", tb)] = marks[("C2ev", tb, H // 512 - 1)]
            elif "c2tr" in KSKIP:
                marks[("C2evall", tb)] = ("vch", cnt["vch"])
            if "c2ihw" not in KSKIP:
                waitm("gpsimd", ("C2evall", tb) if "c2st" not in KSKIP else ("C2a", tb))
                for hkg in range(HKG):
                    dma("gpsimd", ih_a[(hkg * NB + tb) * 128:(hkg * NB + tb + 1) * 128, :],
                        ihstage[:, hkg * 1024:(hkg + 1) * 1024], f"ihw{tb % 2}")
            marks[("C2ihw", tb)] = (f"ihw{tb % 2}", cnt[f"ihw{tb % 2}"])
        if barrier():
            return

        waitv("gpsimd", "ihw0", cnt["ihw0"])
        waitv("gpsimd", "ihw1", cnt["ihw1"])
        waitm("sync", ("AGw",))
        for hkg in range(HKG):
            wp = hkg % 2
            if hkg >= 2:
                waitm("sync", ("C3mm", hkg - 2))
            for hk2 in range(8):
                dma("sync", wdq[wp][:, hk2 * D:(hk2 + 1) * D],
                    wdqa[(hkg * 8 + hk2) * 128:(hkg * 8 + hk2 + 1) * 128, :],
                    f"wdl{wp}", mark=("C3q", hkg, hk2))
            for tb in range(NB):
                idx = hkg * NB + tb
                ip = idx % 2
                if idx >= 2:
                    waitm("gpsimd", ("C3mmI", idx - 2))
                dma("gpsimd", ihsb[ip][:], ih_a[(hkg * NB + tb) * 128:(hkg * NB + tb + 1) * 128, :],
                    f"ihr{ip}", mark=("C3ihr", idx))
                for ds in range(NDS):
                    yp = (idx * NDS + ds) % 2
                    pys = [psA[:], psB[:]] if yp == 0 else [psC[:], psD[:]]
                    waitm("tensor", ("C3ihr", idx))
                    waitm("tensor", ("C3q", hkg, 7))
                    if (idx * NDS + ds) >= 2:
                        waitm("tensor", ("C3yac", idx * NDS + ds - 2))
                    nhalf = 2 if D >= 1024 else 1
                    for k2 in range(8):
                        for hf in range(nhalf):
                            c0 = ds * 1024 + hf * DW
                            opT(lambda e, ip=ip, k2=k2, hf=hf, wp=wp, c0=c0, pys=pys: e.matmul(
                                pys[hf][:, :DW], ihsb[ip][:, k2 * 128:(k2 + 1) * 128],
                                wdq[wp][:, k2 * D + c0: k2 * D + c0 + DW],
                                start=(k2 == 0), stop=(k2 == 7)),
                                "mm3" if (k2 == 7 and hf == nhalf - 1) else None,
                                mark=("C3mmP", idx, ds) if (k2 == 7 and hf == nhalf - 1) else None)
                    waitm("vector", ("C3mmP", idx, ds))
                    for hf in range(nhalf):
                        c0 = tb * D + ds * 1024 + hf * DW
                        if hkg == 0:
                            opE("vector", lambda e, c0=c0, hf=hf, pys=pys: e.tensor_copy(
                                yacc[:, c0:c0 + DW], pys[hf][:, :DW]),
                                mark=("C3yac", idx * NDS + ds) if hf == nhalf - 1 else None)
                        else:
                            opE("vector", lambda e, c0=c0, hf=hf, pys=pys: e.tensor_tensor(
                                yacc[:, c0:c0 + DW], yacc[:, c0:c0 + DW], pys[hf][:, :DW], ALU.add),
                                mark=("C3yac", idx * NDS + ds) if hf == nhalf - 1 else None)
                marks[("C3mmI", idx)] = ("vch", cnt["vch"])
            marks[("C3mm", hkg)] = ("mm3", cnt["mm3"])
        if barrier():
            return

        waitv("gpsimd", "sw0", cnt["sw0"])
        waitv("gpsimd", "sw1", cnt["sw1"])
        opE("vector", lambda e: e.tensor_scalar(scd[:, nbs], sinvh[:, nbs], wclip[:, 2:3], -1.0, ALU.mult, ALU.mult),
            mark=("Dscd",))
        for tb in range(NB):
            p = tb % 2
            if tb >= 2:
                waitm("sync", ("Dsg", tb - 2))
            dma("sync", s_tm[p][:], s_a[tb * 128:(tb + 1) * 128, :], f"sr{p}")
            dma("sync", p_tm[p][:], p_a[tb * 128:(tb + 1) * 128, :], f"sr{p}",
                mark=("Dsr", tb))
            dma("gpsimd", xod[p][:], xo[tb * 128:(tb + 1) * 128, :], f"xol{p}", mark=("Dxo", tb))
            waitm("scalar", ("Dscd",))
            if tb >= 1:
                waitm("scalar", ("Dsg", tb - 1))
            opE("scalar", lambda e, tb=tb: e.activation(
                gfin[:], yacc[:, tb * D:(tb + 1) * D], AF.Silu, scale=scd[:, tb:tb + 1]),
                mark=("Dgf", tb))
            waitm("vector", ("Dgf", tb))
            waitm("vector", ("Dsr", tb))
            opE("vector", lambda e, p=p: e.tensor_tensor(sgb[:], p_tm[p][:], Hb[:, 0:E], ALU.mult))
            opE("vector", lambda e, p=p: e.tensor_tensor(s_tm[p][:], s_tm[p][:], sgb[:], ALU.add))
            opE("vector", lambda e, p=p: e.tensor_tensor(sgb[:], s_tm[p][:], gfin[:], ALU.mult),
                mark=("Dsg", tb))
            waitm("vector", ("Dxo", tb))
            if tb >= 2:
                waitm("vector", ("Dout", tb - 2))
            opE("vector", lambda e, p=p: e.tensor_tensor(outsb[p][:], sgb[:], xod[p][:], ALU.add),
                mark=("Dob", tb))
            waitm("gpsimd", ("Dob", tb))
            dma("gpsimd", out[tb * 128:(tb + 1) * 128, :], outsb[p][:], f"outw{p}", mark=("Dout", tb))
        waitv("gpsimd", "outw0", cnt["outw0"])
        waitv("gpsimd", "outw1", cnt["outw1"])

    from contextlib import ExitStack as ES2
    with nc.Block() as block:
        @block.sync
        def _(e):
            describe("sync", e, {})

        @block.vector
        def _(e):
            describe("vector", e, {})

        @block.scalar
        def _(e):
            describe("scalar", e, {})

        @block.gpsimd
        def _(e):
            describe("gpsimd", e, {})

        @block.tensor
        def _(e):
            describe("tensor", e, {})

    stack.close()
    return nc


_NC_CACHE = {}


def _get_nc(S, D, H):
    key = (S, D, H)
    if key not in _NC_CACHE:
        _NC_CACHE[key] = build_nc(S, D, H)
    return _NC_CACHE[key]


def _pair_permute_wt(W, E):
    WT = np.ascontiguousarray(W.T)
    cols = []
    for cc in range(E // 128):
        cols.append(WT[:, cc * 128:(cc + 1) * 128])
        cols.append(WT[:, E + cc * 128: E + (cc + 1) * 128])
    return np.ascontiguousarray(np.concatenate(cols, axis=1))


def run(x, W, wgate, wup, wdown):
    B, S, D = x.shape
    H = wgate.shape[0]
    E = D
    nc = _get_nc(S, D, H)
    x = np.ascontiguousarray(np.asarray(x, np.float32))
    wtp = _pair_permute_wt(np.asarray(W, np.float32), E)
    wgt = np.ascontiguousarray(np.asarray(wgate, np.float32).T)
    wut = np.ascontiguousarray(np.asarray(wup, np.float32).T)
    wdt = np.ascontiguousarray(np.asarray(wdown, np.float32).T)
    TOK = S // 4
    in_maps = []
    for cid in range(8):
        b, j = cid // 4, cid % 4
        m = np.zeros(16, np.float32)
        for r in range(8):
            m[r] = 1.0 if (r // 4 == b and r % 4 < j) else 0.0
            m[8 + r] = 1.0 - m[r]
        shw = wtp.shape[0] // 8
        shd = wdt.shape[0] // 8
        in_maps.append({
            "xown": np.ascontiguousarray(x[b, j * TOK:(j + 1) * TOK]),
            "wtsh": np.ascontiguousarray(wtp[cid * shw:(cid + 1) * shw]),
            "wgsh": np.ascontiguousarray(wgt[cid * shw:(cid + 1) * shw]),
            "wush": np.ascontiguousarray(wut[cid * shw:(cid + 1) * shw]),
            "wdsh": np.ascontiguousarray(wdt[cid * shd:(cid + 1) * shd]),
            "cmask": np.ascontiguousarray(np.tile(m, (128, 1))),
        })
    res = run_bass_kernel_spmd(nc, in_maps, core_ids=list(range(8)))
    outs = [np.asarray(res.results[c]["out"]) for c in range(8)]
    full = np.stack([np.concatenate(outs[0:4], axis=0),
                     np.concatenate(outs[4:8], axis=0)], axis=0)
    return full.astype(np.float32)


def kernel(L0_attn_x, L0_attn_W, L0_ffn_w_gate, L0_ffn_w_up, L0_ffn_w_down):
    return run(L0_attn_x, L0_attn_W, L0_ffn_w_gate, L0_ffn_w_up, L0_ffn_w_down)


# version 33 (final)
# speedup vs baseline: 1.1017x; 1.1017x over previous
import os
import numpy as np
import concourse.bass as bass
import concourse.mybir as mybir
from concourse.bass_utils import run_bass_kernel_spmd

F32 = mybir.dt.float32
BF16 = mybir.dt.bfloat16
I8 = mybir.dt.int8
I32 = mybir.dt.int32
AF = mybir.ActivationFunctionType
ALU = mybir.AluOpType
AX = mybir.AxisListType

ENGS = ("sync", "vector", "scalar", "gpsimd", "tensor")


def build_nc(S, D, H):
    E = D
    TOK = S // 4
    NB = TOK // 128
    KC = D // 128
    CC = E // 128
    NPG = CC // 2
    GT = min(512, S // 4)
    NG = (S // 4) // GT
    GB = GT // 128
    NSL = H // 512
    HK = H // 128
    HKG = HK // 8
    NDS = max(1, D // 1024)
    DW = min(512, D)
    WSF = 1024

    nc = bass.Bass("TRN2", target_bir_lowering=False)

    SHD_W = D // 8
    SHD_D = H // 8
    xo_d = nc.dram_tensor("xown", [TOK, D], F32, kind="ExternalInput")
    wtsh_d = nc.dram_tensor("wtsh", [SHD_W, 2 * E], F32, kind="ExternalInput")
    wgsh_d = nc.dram_tensor("wgsh", [SHD_W, H], F32, kind="ExternalInput")
    wush_d = nc.dram_tensor("wush", [SHD_W, H], F32, kind="ExternalInput")
    wdsh_d = nc.dram_tensor("wdsh", [SHD_D, D], F32, kind="ExternalInput")
    cm_d = nc.dram_tensor("cmask", [128, 16], F32, kind="ExternalInput")
    out_d = nc.dram_tensor("out", [TOK, D], F32, kind="ExternalOutput")
    wtbfsh_d = nc.dram_tensor("wtbfsh", [SHD_W, 2 * E], BF16, kind="Internal")
    wgqsh_d = nc.dram_tensor("wgqsh", [SHD_W, H], BF16, kind="Internal")
    wuqsh_d = nc.dram_tensor("wuqsh", [SHD_W, H], BF16, kind="Internal")
    wdqsh_d = nc.dram_tensor("wdqsh", [SHD_D, D], BF16, kind="Internal")
    wtbf_d = nc.dram_tensor("wtbf", [D, 2 * E], BF16, kind="Internal", addr_space="Shared")
    wgq_d = nc.dram_tensor("wgq", [D, H], BF16, kind="Internal", addr_space="Shared")
    wuq_d = nc.dram_tensor("wuq", [D, H], BF16, kind="Internal", addr_space="Shared")
    wdq_d = nc.dram_tensor("wdq_g", [H, D], BF16, kind="Internal", addr_space="Shared")
    s_d = nc.dram_tensor("s_scr", [TOK, E], F32, kind="Internal")
    p_d = nc.dram_tensor("p_scr", [TOK, E], F32, kind="Internal")
    ccin_d = nc.dram_tensor("cc_in", [128, 32], F32, kind="Internal")
    ccout_d = nc.dram_tensor("cc_out", [8 * 128, 32], F32, kind="Internal", addr_space="Shared")
    h_d = nc.dram_tensor("h_scr", [TOK, H], F32, kind="Internal")
    ih_d = nc.dram_tensor("ih_scr", [HKG * NB * 128, 8 * 128], BF16, kind="Internal")

    xo = xo_d.ap(); wtsh = wtsh_d.ap(); wgsh = wgsh_d.ap()
    wush = wush_d.ap(); wdsh = wdsh_d.ap(); cm = cm_d.ap(); out = out_d.ap()
    wtbf = wtbf_d.ap(); s_a = s_d.ap(); h_a = h_d.ap(); ih_a = ih_d.ap()
    p_a = p_d.ap(); wgq = wgq_d.ap(); wuq = wuq_d.ap(); wdqa = wdq_d.ap()
    wtbfsh = wtbfsh_d.ap(); wgqsh = wgqsh_d.ap(); wuqsh = wuqsh_d.ap()
    wdqsh = wdqsh_d.ap()

    def sb(name, shape, dt):
        return nc.alloc_sbuf_tensor(name, shape, dt)

    ident = sb("ident", [128, 128], BF16)
    identf = sb("identf", [128, 128], F32)
    identi = sb("identi", [128, 128], I32)
    ones_col = sb("ones_col", [128, 1], F32)
    ones_row = sb("ones_row", [1, 128], F32)
    c5 = sb("c5", [128, 1], F32)
    c6 = sb("c6", [128, 1], F32)
    cm_sb = sb("cm_sb", [128, 16], F32)
    Hcar = sb("Hcar", [128, 16], F32)
    Htr = sb("Htr", [16, 128], F32)
    Rrow = sb("Rrow", [1, 128], F32)
    cstg = sb("cstg", [128, 32], F32)
    ccg = sb("ccg", [128, 256], F32)
    tmpH = sb("tmpH", [128, 16], F32); tmpP = sb("tmpP", [128, 16], F32)
    tmpQ = sb("tmpQ", [128, 16], F32)
    carryP = sb("carryP", [128, 16], F32)
    STAT = lambda n: sb(n, [128, max(NB, 4)], F32)
    musum = STAT("musum"); sumsq = STAT("sumsq"); maxx = STAT("maxx"); minn = STAT("minn")
    mu = STAT("mu"); rho = STAT("rho"); rr = STAT("rr"); alpha = STAT("alpha")
    sinv = STAT("sinv"); tmpa = STAT("tmpa"); tmpb = STAT("tmpb"); tmpc = STAT("tmpc")
    sumsqh = STAT("sumsqh"); maxh = STAT("maxh"); alphah = STAT("alphah"); sinvh = STAT("sinvh")
    hsq1 = STAT("hsq1"); hsq2 = STAT("hsq2")
    scg = STAT("scg"); scu = STAT("scu"); scd = STAT("scd")
    carry = sb("carry", [128, 16], F32)
    wacc = sb("wacc", [128, 64], F32)
    wbar = sb("wbar", [128, 4], F32); wclip = sb("wclip", [128, 4], F32)
    winv = sb("winv", [128, 4], F32)
    wth = sb("wth", [128, 4], F32); wmth = sb("wmth", [128, 4], F32)
    Hb = sb("Hb", [128, max(E, 4 * 128)], F32)
    ixfm = sb("ixfm", [128, KC * TOK], BF16)
    yacc = sb("yacc", [128, NB * D], F32)

    ARENA_BYTES = 98 * 1024
    arena = nc.alloc_sbuf_tensor("arena", [128, ARENA_BYTES], I8)
    ar_ap = {F32: arena.bitcast(F32).ap(), BF16: arena.bitcast(BF16).ap(),
             I8: arena.bitcast(I8).ap()}
    ELSZ = {F32: 4, BF16: 2, I8: 1, I32: 4}

    def trans_alloc():
        p = [0]
        def t(name, shape, dt):
            at = p[0]
            w = shape[1] * ELSZ[dt]
            p[0] = at + ((w + 31) // 32) * 32
            assert p[0] <= ARENA_BYTES, (name, p[0])
            return ar_ap[dt][:, at // ELSZ[dt]: at // ELSZ[dt] + shape[1]]
        return t

    t = trans_alloc()
    xa = [t(f"xa{i}", [128, D], F32) for i in range(2)]
    sqscr = t("sqscr", [128, max(D, WSF)], BF16)
    i8a = t("i8a", [128, D], I8)
    ibf = [t(f"ibf{i}", [128, D], BF16) for i in range(2)]
    ybufA = t("ybufA", [128, D], F32)
    rbufA = t("rbufA", [128, D], F32)
    cbufA = t("cbufA", [128, D], BF16)
    wtc_f = [t(f"wtcf{i}", [128, min(2048, 2 * E)], F32) for i in range(2)]
    wtc_b = t("wtcb", [128, min(2048, 2 * E)], BF16)
    ws_f = [t(f"wsf{i}", [128, 2048], F32) for i in range(2)]
    endA = None

    t = trans_alloc()
    _xg0 = t("xg0", [128, D], F32)
    xg = [_xg0, _xg0]
    xn_tm = [t(f"xntm{i}", [128, D], BF16) for i in range(2)]
    xn_fm = t("xnfm", [128, KC * GT], BF16)
    wts = [t(f"wts{i}", [128, KC * 512], BF16) for i in range(2)]
    qf = [t(f"qf{i}", [128, 512], F32) for i in range(2)]
    qb = [t(f"qb{i}", [128, 512], BF16) for i in range(2)]
    c1q = t("c1q", [128, 512], BF16)
    gsb = [t(f"gsb{i}", [128, GT], F32) for i in range(2)]
    qsb = [t(f"qsb{i}", [128, GT], F32) for i in range(2)]
    hscan = [t(f"hscan{i}", [128, GT], F32) for i in range(2)]
    pscan = [t(f"pscan{i}", [128, GT], F32) for i in range(2)]
    s_stg = [t(f"sstg{i}", [128, GT], F32) for i in range(2)]
    p_stg = [t(f"pstg{i}", [128, GT], F32) for i in range(2)]

    t = trans_alloc()
    wqg = [t(f"wqg{i}", [128, KC * 512], BF16) for i in range(2)]
    wqu = [t(f"wqu{i}", [128, KC * 512], BF16) for i in range(2)]
    sgate = [t(f"sgate{i}", [128, 512], F32) for i in range(2)]
    hst = [t(f"hst{i}", [128, 512], F32) for i in range(2)]

    t = trans_alloc()
    hfull = [t(f"hfull{i}", [128, H], F32) for i in range(2)]
    ihstage = t("ihstage", [128, HK * 128], BF16)
    i8h = t("i8h", [128, 512], I8)
    ibfh = [t(f"ibfh{i}", [128, 512], BF16) for i in range(2)]
    sqh_scr = t("sqh_scr", [128, H // 2], BF16)
    ybufH = t("ybufH", [128, 512], F32)
    rbufH = t("rbufH", [128, 512], F32)
    cbufH = t("cbufH", [128, 512], BF16)

    t = trans_alloc()
    wdq = [t(f"wdq{i}", [128, 8 * D], BF16) for i in range(2)]
    ihsb = [t(f"ihsb{i}", [128, 8 * 128], BF16) for i in range(2)]

    t = trans_alloc()
    s_tm = [t(f"stm{i}", [128, E], F32) for i in range(2)]
    p_tm = [t(f"ptm{i}", [128, E], F32) for i in range(2)]
    gfin = t("gfin", [128, D], F32)
    sgb = t("sgb", [128, D], F32)
    xod = [t(f"xod{i}", [128, D], F32) for i in range(2)]
    outsb = [t(f"outsb{i}", [128, D], F32) for i in range(2)]

    psA = nc.alloc_psum_tensor("psA", [128, 512], F32)
    psB = nc.alloc_psum_tensor("psB", [128, 512], F32)
    psC = nc.alloc_psum_tensor("psC", [128, 512], F32)
    psD = nc.alloc_psum_tensor("psD", [128, 512], F32)
    ptrA = nc.alloc_psum_tensor("ptrA", [128, 1024], BF16)
    ptrB = nc.alloc_psum_tensor("ptrB", [128, 1024], BF16)
    psS = nc.alloc_psum_tensor("psS", [128, 512], F32)
    psS2 = nc.alloc_psum_tensor("psS2", [128, 512], F32)

    sem_names = [
        "ph", "vch", "sch", "gch", "cml",
        "xa0", "xa1", "xbl0", "xbl1", "wtl0", "wtl1", "wtw",
        "wtsl0", "wtsl1", "wsl0", "wsl1", "hw0", "hw1", "qbw0", "qbw1",
        "hr0", "hr1", "ihw0", "ihw1", "ihr0", "ihr1", "wdl0", "wdl1", "c1l0", "c1l1",
        "sr0", "sr1", "xol0", "xol1", "outw0", "outw1", "sw0", "sw1", "ccs", "ccc", "ccd",
        "tpA", "tpB", "tpC", "tpD", "mmB", "mm1", "mm3",
    ]
    sems = {}
    from contextlib import ExitStack
    stack = ExitStack()
    for n in sem_names:
        sems[n] = stack.enter_context(nc.semaphore(n))

    WSTW = 2048
    wstat_tiles = []
    wranges = []
    for m, (ap_, R, C) in enumerate([(wgsh, SHD_W, H), (wush, SHD_W, H), (wdsh, SHD_D, D)]):
        lo = len(wstat_tiles)
        for r in range(max(1, R // 128)):
            for c0 in range(0, C, WSTW):
                wstat_tiles.append((ap_, r * 128, c0, min(WSTW, C - c0), min(128, R)))
        wranges.append((lo, len(wstat_tiles)))
    wquant_tiles = []
    for m, (sap, dap, R, C) in enumerate([(wgsh, wgqsh, SHD_W, H), (wush, wuqsh, SHD_W, H),
                                          (wdsh, wdqsh, SHD_D, D)]):
        for r in range(max(1, R // 128)):
            for c0 in range(0, C, 512):
                wquant_tiles.append((sap, dap, r * 128, c0, min(512, C - c0),
                                     min(128, R), m))

    CH = {"vector": "vch", "scalar": "sch", "gpsimd": "gch"}

    PHLIM = int(os.environ.get("KPHL", "99"))
    KSKIP = set(os.environ.get("KSKIP", "").split(","))

    def describe(real, eng, ctx):
        cnt = {n: 0 for n in sem_names}
        marks = {}
        phase_no = [0]

        def me(e):
            return e == real

        def opE(e, fn, mark=None):
            s = CH[e]
            if me(e):
                if cnt[s] > 0:
                    eng.wait_ge(sems[s], cnt[s])
                fn(eng).then_inc(sems[s], 1)
            cnt[s] += 1
            if mark is not None:
                marks[mark] = (s, cnt[s])

        def opT(fn, sem=None, mark=None):
            if me("tensor"):
                ins = fn(eng)
                if sem is not None:
                    ins.then_inc(sems[sem], 1)
            if sem is not None:
                cnt[sem] += 1
                if mark is not None:
                    marks[mark] = (sem, cnt[sem])

        def dma(e, dst, src, sem, mark=None):
            if me(e):
                eng.dma_start(dst, src).then_inc(sems[sem], 16)
            cnt[sem] += 16
            if mark is not None:
                marks[mark] = (sem, cnt[sem])

        def waitm(e, mark):
            s, v = marks[mark]
            if me(e) and v > 0:
                eng.wait_ge(sems[s], v)

        def waitv(e, sem, v):
            if me(e) and v > 0:
                eng.wait_ge(sems[sem], v)

        def waitcur(e, sem):
            waitv(e, sem, cnt[sem])

        GP_DMA = ["cml", "xa0", "xa1", "xbl0", "xbl1", "wtw", "hw0", "hw1",
                  "ihw0", "ihw1",
                  "outw0", "outw1", "sw0", "sw1"]
        GP_DMA += ["qbw0", "qbw1"]
        SY_DMA = ["wtl0", "wtl1", "wsl0", "wsl1", "c1l0", "c1l1",
                  "wdl0", "wdl1", "sr0", "sr1", "hr0", "hr1"]
        SC_DMA = ["wtsl0", "wtsl1", "ihr0", "ihr1", "xol0", "xol1"]
        ENG_DMA = {"gpsimd": GP_DMA, "sync": SY_DMA, "scalar": SC_DMA}

        def finish_early():
            for tb in range(NB):
                dma("gpsimd", out[tb * 128:(tb + 1) * 128, :], xod[0][:], f"outw{tb % 2}")
            waitv("gpsimd", "outw0", cnt["outw0"])
            waitv("gpsimd", "outw1", cnt["outw1"])

        def barrier():
            cnt["ph"] += 5
            if real in ENGS:
                drains = ENG_DMA.get(real, [])
                for s_ in drains:
                    if cnt[s_] > 0:
                        eng.wait_ge(sems[s_], cnt[s_])
                if real in CH and cnt[CH[real]] > 0:
                    eng.wait_ge(sems[CH[real]], cnt[CH[real]])
                eng.sem_inc(sems["ph"], 1)
                eng.wait_ge(sems["ph"], cnt["ph"])
            phase_no[0] += 1
            if phase_no[0] >= PHLIM:
                finish_early()
                return True
            return False

        tstates = {}

        def trstream(tag, tpsem, pairs, ident_ap, banks, gsz, src_mark, ev="vector"):
            st = tstates.setdefault(tag, {"grp": 0, "bmark": {0: None, 1: None}})
            last_ev = None
            i = 0
            while i < len(pairs):
                chunk = pairs[i:i + gsz]
                b = st["grp"] % 2
                bank = banks[b]
                if st["bmark"][b] is not None:
                    waitm("tensor", st["bmark"][b])
                if src_mark is not None:
                    waitm("tensor", src_mark)
                for sl, (srcap, dstap) in enumerate(chunk):
                    opT(lambda e, srcap=srcap, bank=bank, sl=sl, ident_ap=ident_ap:
                        e.transpose(bank[:, sl * 128:(sl + 1) * 128], srcap, ident_ap),
                        tpsem, mark=(tag, "tp", st["grp"]) if sl == len(chunk) - 1 else None)
                waitm(ev, (tag, "tp", st["grp"]))
                for sl, (srcap, dstap) in enumerate(chunk):
                    if ev == "scalar":
                        opE(ev, lambda e, dstap=dstap, bank=bank, sl=sl:
                            e.activation(dstap, bank[:, sl * 128:(sl + 1) * 128], AF.Copy),
                            mark=(tag, "ev", st["grp"]) if sl == len(chunk) - 1 else None)
                    else:
                        opE(ev, lambda e, dstap=dstap, bank=bank, sl=sl:
                            e.tensor_copy(dstap, bank[:, sl * 128:(sl + 1) * 128]),
                            mark=(tag, "ev", st["grp"]) if sl == len(chunk) - 1 else None)
                st["bmark"][b] = (tag, "ev", st["grp"])
                last_ev = (tag, "ev", st["grp"])
                st["grp"] += 1
                i += gsz
            return marks[last_ev]

        opE("gpsimd", lambda e: e.iota(identi[:], [[1, 128]], base=0, channel_multiplier=-1))
        opE("gpsimd", lambda e: e.tensor_scalar(ident[:], identi[:], 0, None, ALU.is_equal))
        opE("gpsimd", lambda e: e.tensor_scalar(identf[:], identi[:], 0, None, ALU.is_equal))
        opE("gpsimd", lambda e: e.memset(ones_col[:], 1.0))
        opE("gpsimd", lambda e: e.memset(ones_row[:], 1.0))
        opE("gpsimd", lambda e: e.memset(carry[:], 0.0))
        opE("gpsimd", lambda e: e.memset(carryP[:], 1.0))
        opE("gpsimd", lambda e: e.memset(Hcar[:], 0.0))
        opE("gpsimd", lambda e: e.memset(cstg[:], 0.0))
        opE("gpsimd", lambda e: e.memset(wacc[:], 0.0))
        opE("gpsimd", lambda e: e.memset(c5[:], 1e-5))
        opE("gpsimd", lambda e: e.memset(c6[:], 1e-6))
        dma("gpsimd", cm_sb[:], cm, "cml")
        waitv("gpsimd", "cml", 16)
        if barrier():
            return

        for tb in range(NB):
            p = tb % 2
            if tb >= 2:
                waitm("gpsimd", ("Ava", tb - 2))
                waitm("gpsimd", ("Asa", tb - 2))
            dma("gpsimd", xa[p][:], xo[tb * 128:(tb + 1) * 128, :], f"xa{p}", mark=("Axa", tb))
            waitm("vector", ("Axa", tb))
            opE("vector", lambda e, p=p, tb=tb: e.reduce_sum(musum[:, tb:tb + 1], xa[p][:], AX.X))
            opE("vector", lambda e, p=p, tb=tb: e.reduce_max(maxx[:, tb:tb + 1], xa[p][:], AX.X))
            opE("vector", lambda e, p=p, tb=tb: e.tensor_reduce(
                minn[:, tb:tb + 1], xa[p][:], AX.X, ALU.min), mark=("Ava", tb))
            waitm("scalar", ("Axa", tb))
            opE("scalar", lambda e, p=p, tb=tb: e.activation(
                sqscr[:, :D], xa[p][:], AF.Square, accum_out=sumsq[:, tb:tb + 1]),
                mark=("Asa", tb))
        waitm("vector", ("Ava", NB - 1))
        waitm("vector", ("Asa", NB - 1))
        nbs = slice(0, NB)
        opE("vector", lambda e: e.tensor_scalar(mu[:, nbs], musum[:, nbs], 1.0 / D, None, ALU.mult))
        opE("vector", lambda e: e.tensor_scalar(tmpa[:, nbs], sumsq[:, nbs], 1.0 / D, None, ALU.mult))
        opE("vector", lambda e: e.tensor_tensor(tmpb[:, nbs], mu[:, nbs], mu[:, nbs], ALU.mult))
        opE("vector", lambda e: e.tensor_tensor(tmpa[:, nbs], tmpa[:, nbs], tmpb[:, nbs], ALU.subtract),
            mark=("Avar",))
        waitm("scalar", ("Avar",))
        opE("scalar", lambda e: e.activation(tmpb[:, nbs], tmpa[:, nbs], AF.Sqrt, bias=c5[:]),
            mark=("Asq1",))
        waitm("vector", ("Asq1",))
        opE("vector", lambda e: e.reciprocal(rho[:, nbs], tmpb[:, nbs]))
        opE("vector", lambda e: e.tensor_scalar(tmpb[:, nbs], tmpa[:, nbs], 1e-5, None, ALU.add))
        opE("vector", lambda e: e.reciprocal(tmpc[:, nbs], tmpb[:, nbs]))
        opE("vector", lambda e: e.tensor_tensor(tmpb[:, nbs], tmpa[:, nbs], tmpc[:, nbs], ALU.mult),
            mark=("Aq2",))
        waitm("scalar", ("Aq2",))
        opE("scalar", lambda e: e.activation(tmpc[:, nbs], tmpb[:, nbs], AF.Sqrt, bias=c6[:]),
            mark=("Asq2",))
        waitm("vector", ("Asq2",))
        opE("vector", lambda e: e.reciprocal(rr[:, nbs], tmpc[:, nbs]))
        opE("vector", lambda e: e.tensor_tensor(tmpa[:, nbs], maxx[:, nbs], mu[:, nbs], ALU.subtract))
        opE("vector", lambda e: e.tensor_tensor(tmpb[:, nbs], mu[:, nbs], minn[:, nbs], ALU.subtract))
        opE("vector", lambda e: e.tensor_tensor(tmpa[:, nbs], tmpa[:, nbs], tmpb[:, nbs], ALU.max))
        opE("vector", lambda e: e.tensor_tensor(tmpa[:, nbs], tmpa[:, nbs], rho[:, nbs], ALU.mult))
        opE("vector", lambda e: e.tensor_tensor(tmpa[:, nbs], tmpa[:, nbs], rr[:, nbs], ALU.mult))
        opE("vector", lambda e: e.tensor_scalar(tmpa[:, nbs], tmpa[:, nbs], 1e-5, None, ALU.max))
        opE("vector", lambda e: e.tensor_scalar(sinv[:, nbs], tmpa[:, nbs], 1.0 / 127.0, None, ALU.mult))
        opE("vector", lambda e: e.reciprocal(tmpb[:, nbs], tmpa[:, nbs]))
        opE("vector", lambda e: e.tensor_tensor(tmpc[:, nbs], rho[:, nbs], rr[:, nbs], ALU.mult))
        opE("vector", lambda e: e.tensor_tensor(tmpc[:, nbs], tmpc[:, nbs], tmpb[:, nbs], ALU.mult))
        opE("vector", lambda e: e.tensor_scalar(alpha[:, nbs], tmpc[:, nbs], 127.0, None, ALU.mult),
            mark=("Astat",))

        wtcW = min(2048, 2 * E)
        wtcR = min(128, SHD_W)
        wtc_tiles = [(k, c0) for k in range(max(1, SHD_W // 128))
                     for c0 in range(0, 2 * E, wtcW)]
        wtc_per = (len(wtc_tiles) + NB - 1) // NB
        ws_per = (len(wstat_tiles) + NB - 1) // NB
        for tb in range(NB):
            p = tb % 2
            waitm("gpsimd", ("Astat",))
            if tb >= 2:
                waitm("gpsimd", ("Aq", tb - 2))
            dma("gpsimd", xa[p][:], xo[tb * 128:(tb + 1) * 128, :], f"xa{p}", mark=("Axa2", tb))
            waitm("vector", ("Axa2", tb))
            opE("vector", lambda e, p=p, tb=tb: e.tensor_scalar(
                ybufA[:], xa[p][:], mu[:, tb:tb + 1], alpha[:, tb:tb + 1], ALU.subtract, ALU.mult))
            opE("vector", lambda e: e.tensor_scalar(i8a[:], ybufA[:], 1.0, None, ALU.mult))
            opE("vector", lambda e: e.tensor_tensor(rbufA[:], ybufA[:], i8a[:], ALU.subtract))
            opE("vector", lambda e: e.tensor_scalar(cbufA[:], rbufA[:], 0.5, None, ALU.is_gt))
            opE("vector", lambda e: e.scalar_tensor_tensor(
                ybufA[:], rbufA[:], -0.5, cbufA[:], ALU.is_lt, ALU.subtract))
            if tb >= 2:
                waitm("vector", ("AevA", tb - 2))
            opE("vector", lambda e, p=p: e.tensor_tensor(
                ibf[p][:], i8a[:], ybufA[:], ALU.subtract), mark=("Aq", tb))
            pairsA = [(ibf[p][:, c * 128:(c + 1) * 128],
                       ixfm[:, c * TOK + tb * 128: c * TOK + (tb + 1) * 128])
                      for c in range(KC)]
            marks[("AevA", tb)] = trstream(
                "A", "tpA", pairsA, ident[:], [ptrA, ptrB], 8, ("Aq", tb))
            for i in range(tb * wtc_per, min((tb + 1) * wtc_per, len(wtc_tiles))):
                k, c0 = wtc_tiles[i]
                pw = i % 2
                if i >= 2:
                    waitm("sync", ("Awtv", i - 2))
                dma("sync", wtc_f[pw][:wtcR, :wtcW],
                    wtsh[k * 128:k * 128 + wtcR, c0:c0 + wtcW],
                    f"wtl{pw}", mark=("Awtl", i))
                waitm("vector", ("Awtl", i))
                if i >= 1:
                    waitm("vector", ("Awtw", i - 1))
                opE("vector", lambda e, pw=pw: e.tensor_copy(
                    wtc_b[:wtcR, :wtcW], wtc_f[pw][:wtcR, :wtcW]), mark=("Awtv", i))
                waitm("gpsimd", ("Awtv", i))
                dma("gpsimd", wtbfsh[k * 128:k * 128 + wtcR, c0:c0 + wtcW],
                    wtc_b[:wtcR, :wtcW], "wtw", mark=("Awtw", i))
            for i in range(tb * ws_per, min((tb + 1) * ws_per, len(wstat_tiles))):
                ap_, r0, c0, fd, nr = wstat_tiles[i]
                pw = i % 2
                if i >= 2:
                    waitm("sync", ("Wab", i - 2))
                dma("sync", ws_f[pw][:nr, :fd], ap_[r0:r0 + nr, c0:c0 + fd],
                    f"wsl{pw}", mark=("Wsl", i))
                waitm("scalar", ("Wsl", i))
                opE("scalar", lambda e, pw=pw, fd=fd, nr=nr, i=i: e.activation(
                    sqscr[:nr, :fd], ws_f[pw][:nr, :fd], AF.Abs, accum_out=wacc[:nr, i:i + 1]),
                    mark=("Wab", i))
        if barrier():
            return

        waitv("gpsimd", "wtw", cnt["wtw"])
        if me("gpsimd"):
            eng.collective_compute(
                "AllGather", ALU.bypass, replica_groups=[list(range(8))],
                ins=[wtbfsh_d.ap().opt()], outs=[wtbf_d.ap().opt()]).then_inc(sems["ccs"], 1)
        cnt["ccs"] += 1
        marks[("AGwt",)] = ("ccs", cnt["ccs"])
        opE("gpsimd", lambda e: e.memset(cstg[:], 0.0), mark=("Cst0",))
        for m in range(3):
            lo, hi = wranges[m]
            waitm("vector", ("Wab", hi - 1))
            if m == 0:
                waitm("vector", ("Cst0",))
            opE("vector", lambda e, m=m, lo=lo, hi=hi: e.reduce_sum(
                cstg[:, m:m + 1], wacc[:, lo:hi], AX.X), mark=("Wsum", m))
        waitm("gpsimd", ("Wsum", 2))
        dma("gpsimd", ccin_d.ap(), cstg[:], "ccd")
        waitv("gpsimd", "ccd", cnt["ccd"])
        if me("gpsimd"):
            eng.collective_compute(
                "AllGather", ALU.bypass, replica_groups=[list(range(8))],
                ins=[ccin_d.ap().opt()], outs=[ccout_d.ap().opt()]).then_inc(sems["ccs"], 1)
        cnt["ccs"] += 1
        waitv("gpsimd", "ccs", cnt["ccs"])
        for r in range(8):
            dma("gpsimd", ccg[:, r * 32:(r + 1) * 32],
                ccout_d.ap()[r * 128:(r + 1) * 128, :], "ccd",
                mark=("WSg",) if r == 7 else None)
        waitm("vector", ("WSg",))
        opE("vector", lambda e: e.tensor_copy(wbar[:, 0:3], ccg[:, 0:3]))
        for r in range(1, 8):
            opE("vector", lambda e, r=r: e.tensor_tensor(
                wbar[:, 0:3], wbar[:, 0:3], ccg[:, r * 32: r * 32 + 3], ALU.add),
                mark=("WSsum",) if r == 7 else None)
        waitm("tensor", ("WSsum",))
        for m in range(3):
            opT(lambda e, m=m: e.matmul(
                psS[0:1, m:m + 1], ones_col[:], wbar[:, m:m + 1], start=True, stop=True),
                "mm1", mark=("Wmm1", m))
        waitm("vector", ("Wmm1", 2))
        for m in range(3):
            opE("vector", lambda e, m=m: e.tensor_copy(wclip[0:1, m:m + 1], psS[0:1, m:m + 1]),
                mark=("Wcp", m))
        waitm("tensor", ("Wcp", 2))
        for m in range(3):
            opT(lambda e, m=m: e.matmul(
                psS[:, 8 + m:9 + m], ones_row[:], wclip[0:1, m:m + 1], start=True, stop=True),
                "mm1", mark=("Wmm2", m))
        waitm("vector", ("Wmm2", 2))
        opE("vector", lambda e: e.tensor_copy(wbar[:, 0:3], psS[:, 8:11]))
        opE("vector", lambda e: e.tensor_scalar(wbar[:, 0:3], wbar[:, 0:3], 1.0 / (D * H), None, ALU.mult))
        opE("vector", lambda e: e.tensor_scalar(wclip[:, 0:3], wbar[:, 0:3], 1e-5, None, ALU.max))
        opE("vector", lambda e: e.tensor_scalar(wth[:, 0:3], wclip[:, 0:3], 0.5, None, ALU.mult))
        opE("vector", lambda e: e.tensor_scalar(wmth[:, 0:3], wclip[:, 0:3], -0.5, None, ALU.mult))
        opE("vector", lambda e: e.tensor_scalar(scg[:, nbs], sinv[:, nbs], wclip[:, 0:1], -1.0, ALU.mult, ALU.mult))
        opE("vector", lambda e: e.tensor_scalar(scu[:, nbs], sinv[:, nbs], wclip[:, 1:2], -1.0, ALU.mult, ALU.mult),
            mark=("Wready",))

        qt_slots = max(1, (NG * NPG * 2) // 5)
        qt_per = (len(wquant_tiles) + qt_slots - 1) // qt_slots
        qt_next = [0]
        N_G = sum(1 for tle in wquant_tiles if tle[6] < 1)
        N_GU = sum(1 for tle in wquant_tiles if tle[6] < 2)
        NPALL = NG * NPG
        P_G = NPALL // 2
        P_U = (NPALL * 2) // 3

        def fire_ags(pairs, mk):
            waitv("gpsimd", "qbw0", cnt["qbw0"])
            waitv("gpsimd", "qbw1", cnt["qbw1"])
            for (insh, outfull) in pairs:
                if me("gpsimd"):
                    eng.collective_compute(
                        "AllGather", ALU.bypass, replica_groups=[list(range(8))],
                        ins=[insh.ap().opt()], outs=[outfull.ap().opt()]).then_inc(sems["ccs"], 1)
                cnt["ccs"] += 1
            marks[mk] = ("ccs", cnt["ccs"])

        def maybe_fire(p):
            if qt_next[0] >= N_G and p >= P_G and ("AGg",) not in marks:
                fire_ags([(wgqsh_d, wgq_d)], ("AGg",))
            if qt_next[0] >= N_GU and p >= P_U and ("AGu",) not in marks:
                fire_ags([(wuqsh_d, wuq_d)], ("AGu",))

        def quant_slot():
            hi = min(qt_next[0] + qt_per, len(wquant_tiles))
            for i in range(qt_next[0], hi):
                sap, dap, r0, c0, w, nr, m = wquant_tiles[i]
                pw = i % 2
                if i >= 2:
                    waitm("sync", ("Qq", i - 2))
                dma("sync", qf[pw][:nr, :w], sap[r0:r0 + nr, c0:c0 + w],
                    f"wsl{pw}", mark=("Ql", i))
                waitm("vector", ("Ql", i))
                if i >= 2:
                    waitm("vector", ("Qw", i - 2))
                opE("vector", lambda e, pw=pw, nr=nr, w=w, m=m: e.tensor_scalar(
                    c1q[:nr, :w], qf[pw][:nr, :w], wth[:nr, m:m + 1], None, ALU.is_gt))
                opE("vector", lambda e, pw=pw, nr=nr, w=w, m=m: e.scalar_tensor_tensor(
                    qb[pw][:nr, :w], qf[pw][:nr, :w], wmth[:nr, m:m + 1], c1q[:nr, :w],
                    ALU.is_lt, ALU.subtract), mark=("Qq", i))
                waitm("gpsimd", ("Qq", i))
                dma("gpsimd", dap[r0:r0 + nr, c0:c0 + w], qb[pw][:nr, :w], f"qbw{pw}",
                    mark=("Qw", i))
            qt_next[0] = hi

        waitm("scalar", ("AGwt",))
        waitm("sync", ("AGwt",))
        for g in range(NG):
            for tbp in range(GB):
                p = tbp % 2
                bidx = g * GB + tbp
                r0 = g * GT + tbp * 128
                prev_same_par = [t2 for t2 in range(GB) if t2 % 2 == tbp % 2][-1]
                if tbp >= 1:
                    waitm("gpsimd", ("Bxn", g, tbp - 1))
                elif g > 0:
                    waitm("gpsimd", ("Bxn", g - 1, GB - 1))
                dma("gpsimd", xg[p][:], xo[r0:r0 + 128, :], f"xbl{p}", mark=("Bxb2", g, tbp))
                waitm("vector", ("Bxb2", g, tbp))
                if g > 0 or tbp >= 2:
                    pm = (("BevX", g, tbp - 2) if tbp >= 2
                          else ("BevX", g - 1, prev_same_par))
                    waitm("vector", pm)
                opE("vector", lambda e, p=p, bidx=bidx: e.tensor_scalar(
                    xn_tm[p][:], xg[p][:], mu[:, bidx:bidx + 1], rho[:, bidx:bidx + 1],
                    ALU.subtract, ALU.mult), mark=("Bxn", g, tbp))
                if g > 0 and tbp == 0:
                    waitm("vector", ("BmmAll", g - 1))
                pairsB = [(xn_tm[p][:, c * 128:(c + 1) * 128],
                           xn_fm[:, c * GT + tbp * 128: c * GT + (tbp + 1) * 128])
                          for c in range(KC)]
                marks[("BevX", g, tbp)] = trstream(
                    "B", "tpB", pairsB, ident[:], [ptrA, ptrB], 8, ("Bxn", g, tbp))
            marks[("BevG", g)] = marks[("BevX", g, GB - 1)]
            for pg in range(NPG):
                sp = pg % 2
                qeng = "sync" if (g * NPG + pg) >= max(0, NPALL - 8) else "scalar"
                if pg >= 2 or g > 0:
                    pm = (("BmmPG", g, pg - 2) if pg >= 2
                          else ("BmmPG", g - 1, (pg - 2) % NPG))
                    waitm(qeng, pm)
                for k in range(KC):
                    dma(qeng, wts[sp][:, k * 512:(k + 1) * 512],
                        wtbf[k * 128:(k + 1) * 128, pg * 512:(pg + 1) * 512], f"wtsl{sp}")
                marks[("Bwtsl", g, pg)] = (f"wtsl{sp}", cnt[f"wtsl{sp}"])
                for pi in range(2):
                    cc = pg * 2 + pi
                    cp = cc % 2
                    pq = psA if cp == 0 else psC
                    pgm = psB if cp == 0 else psD
                    waitm("tensor", ("Bwtsl", g, pg))
                    waitm("tensor", ("BevG", g))
                    gidx = g * CC + cc
                    if gidx >= 2:
                        waitm("tensor", ("Bqv", gidx - 2))
                        waitm("tensor", ("Bsg", gidx - 2))
                    for k in range(KC):
                        opT(lambda e, k=k, sp=sp, pi=pi, pq=pq: e.matmul(
                            pq[:, :GT], wts[sp][:, k * 512 + pi * 256: k * 512 + pi * 256 + 128],
                            xn_fm[:, k * GT:(k + 1) * GT],
                            start=(k == 0), stop=(k == KC - 1)))
                        opT(lambda e, k=k, sp=sp, pi=pi, pgm=pgm: e.matmul(
                            pgm[:, :GT], wts[sp][:, k * 512 + pi * 256 + 128: k * 512 + pi * 256 + 256],
                            xn_fm[:, k * GT:(k + 1) * GT],
                            start=(k == 0), stop=(k == KC - 1)),
                            "mmB" if k == KC - 1 else None,
                            mark=("BmmP", gidx) if k == KC - 1 else None)
                    waitm("scalar", ("BmmP", gidx))
                    if gidx >= 2:
                        waitm("scalar", ("Bscn", gidx - 2))
                    opE("scalar", lambda e, cp=cp, pgm=pgm: e.activation(
                        gsb[cp][:], pgm[:, :GT], AF.Sigmoid), mark=("Bsg", gidx))
                    waitm("vector", ("BmmP", gidx))
                    if gidx >= 2:
                        waitm("vector", ("Bscn", gidx - 2))
                    opE("vector", lambda e, cp=cp, pq=pq: e.tensor_scalar(
                        qsb[cp][:], pq[:, :GT], 1e-20, None, ALU.max), mark=("Bqv", gidx))
                    waitm("vector", ("Bsg", gidx))
                    if gidx >= 2:
                        waitm("vector", ("BSev", gidx - 2))
                    opE("vector", lambda e, cp=cp, cc=cc: e.tensor_tensor_scan(
                        hscan[cp][:], gsb[cp][:], qsb[cp][:], carry[:, cc:cc + 1],
                        ALU.mult, ALU.add))
                    opE("vector", lambda e, cp=cp, cc=cc: e.tensor_copy(
                        carry[:, cc:cc + 1], hscan[cp][:, GT - 1:GT]))
                    opE("vector", lambda e, cp=cp, cc=cc: e.tensor_tensor_scan(
                        pscan[cp][:], gsb[cp][:], gsb[cp][:], carryP[:, cc:cc + 1],
                        ALU.mult, ALU.bypass))
                    opE("vector", lambda e, cp=cp, cc=cc: e.tensor_copy(
                        carryP[:, cc:cc + 1], pscan[cp][:, GT - 1:GT]), mark=("Bscn", gidx))
                    if gidx >= 2:
                        waitm("vector", ("Bsw", gidx - 2))
                    trstream("BS", "tpD",
                             [(hscan[cp][:, ti * 128:(ti + 1) * 128],
                               s_stg[cp][:, ti * 128:(ti + 1) * 128]) for ti in range(GB)],
                             identf[:], [psS, psS2], 4, ("Bscn", gidx))
                    marks[("BSev", gidx)] = trstream(
                        "BS", "tpD",
                        [(pscan[cp][:, ti * 128:(ti + 1) * 128],
                          p_stg[cp][:, ti * 128:(ti + 1) * 128]) for ti in range(GB)],
                        identf[:], [psS, psS2], 4, ("Bscn", gidx))
                    waitm("gpsimd", ("BSev", gidx))
                    for ti in range(GB):
                        dma("gpsimd",
                            s_a[g * GT + ti * 128: g * GT + (ti + 1) * 128,
                                cc * 128:(cc + 1) * 128],
                            s_stg[cp][:, ti * 128:(ti + 1) * 128], f"sw{cp}")
                        dma("gpsimd",
                            p_a[g * GT + ti * 128: g * GT + (ti + 1) * 128,
                                cc * 128:(cc + 1) * 128],
                            p_stg[cp][:, ti * 128:(ti + 1) * 128],
                            f"sw{cp}", mark=("Bsw", gidx) if ti == GB - 1 else None)
                marks[("BmmPG", g, pg)] = ("mmB", cnt["mmB"])
                quant_slot()
                maybe_fire(g * NPG + pg + 1)
            marks[("BmmAll", g)] = ("mmB", cnt["mmB"])
        quant_slot()
        maybe_fire(NPALL * 2)
        opE("vector", lambda e: e.tensor_copy(cstg[:, 0:CC], carry[:, 0:CC]))
        opE("vector", lambda e: e.tensor_copy(cstg[:, 16:16 + CC], carryP[:, 0:CC]),
            mark=("CCstg",))
        waitm("gpsimd", ("CCstg",))
        dma("gpsimd", ccin_d.ap(), cstg[:], "ccd")
        waitv("gpsimd", "ccd", cnt["ccd"])
        if me("gpsimd"):
            eng.collective_compute(
                "AllGather", ALU.bypass, replica_groups=[list(range(8))],
                ins=[ccin_d.ap().opt()], outs=[ccout_d.ap().opt()]).then_inc(sems["ccc"], 1)
        cnt["ccc"] += 1
        marks[("AGc",)] = ("ccc", cnt["ccc"])
        fire_ags([(wdqsh_d, wdq_d)], ("AGw",))
        if barrier():
            return

        waitm("gpsimd", ("AGc",))
        for r in range(8):
            dma("gpsimd", ccg[:, r * 32:(r + 1) * 32],
                ccout_d.ap()[r * 128:(r + 1) * 128, :], "ccd",
                mark=("CCg",) if r == 7 else None)
        waitm("vector", ("CCg",))
        for r in range(8):
            opE("vector", lambda e, r=r: e.tensor_scalar(
                tmpH[:, 0:CC], ccg[:, r * 32: r * 32 + CC], cm_sb[:, r:r + 1], None, ALU.mult))
            opE("vector", lambda e, r=r: e.tensor_scalar(
                tmpP[:, 0:CC], ccg[:, r * 32 + 16: r * 32 + 16 + CC],
                cm_sb[:, r:r + 1], cm_sb[:, 8 + r:9 + r], ALU.mult, ALU.add))
            opE("vector", lambda e: e.tensor_tensor(tmpQ[:, 0:CC], tmpP[:, 0:CC], Hcar[:, 0:CC], ALU.mult))
            opE("vector", lambda e: e.tensor_tensor(Hcar[:, 0:CC], tmpQ[:, 0:CC], tmpH[:, 0:CC], ALU.add),
                mark=("CCH",) if r == 7 else None)
        waitm("tensor", ("CCH",))
        opT(lambda e: e.transpose(psS[0:CC, 0:128], Hcar[:, 0:CC], identf[:]),
            "tpD", mark=("CCHt",))
        waitm("vector", ("CCHt",))
        opE("vector", lambda e: e.tensor_copy(Htr[0:CC, :], psS[0:CC, 0:128]), mark=("CCHe",))
        for ccx in range(CC):
            if ccx == 0:
                waitm("gpsimd", ("CCHe",))
            else:
                waitm("gpsimd", ("DHm", ccx - 1))
            dma("gpsimd", Rrow[0:1, :], Htr[ccx:ccx + 1, :], "ccd", mark=("DHr", ccx))
            waitm("tensor", ("DHr", ccx))
            bank = [psS, psS2][ccx % 2]
            if ccx >= 2:
                waitm("tensor", ("DHe", ccx - 2))
            opT(lambda e, bank=bank: e.matmul(
                bank[:, 0:128], ones_row[:], Rrow[0:1, :], start=True, stop=True),
                "mmB", mark=("DHm", ccx))
            waitm("vector", ("DHm", ccx))
            opE("vector", lambda e, ccx=ccx, bank=bank: e.tensor_copy(
                Hb[:, ccx * 128:(ccx + 1) * 128], bank[:, 0:128]),
                mark=("DHe", ccx))
        marks[("DHb",)] = marks[("DHe", CC - 1)]

        waitm("sync", ("AGu",))
        for n in range(NSL):
            qp = n % 2
            for (mat, wap, wqbs) in [("g", wgq, wqg), ("u", wuq, wqu)]:
                if n >= 2:
                    waitm("sync", ("C1mmN", n - 2))
                for k in range(KC):
                    dma("sync", wqbs[qp][:, k * 512:(k + 1) * 512],
                        wap[k * 128:(k + 1) * 128, n * 512:(n + 1) * 512], f"c1l{qp}")
                marks[("C1q", n, mat)] = (f"c1l{qp}", cnt[f"c1l{qp}"])
            for tb in range(NB):
                idx = n * NB + tb
                tp = idx % 2
                pg_ = psA if tp == 0 else psC
                pu_ = psB if tp == 0 else psD
                waitm("tensor", ("C1q", n, "u"))
                if idx >= 2:
                    waitm("tensor", ("C1sil", idx - 2))
                    waitm("tensor", ("C1hsv", idx - 2))
                for k in range(KC):
                    opT(lambda e, k=k, tb=tb, pg_=pg_, qp=qp: e.matmul(
                        pg_[:], ixfm[:, k * TOK + tb * 128: k * TOK + (tb + 1) * 128],
                        wqg[qp][:, k * 512:(k + 1) * 512], start=(k == 0), stop=(k == KC - 1)))
                    opT(lambda e, k=k, tb=tb, pu_=pu_, qp=qp: e.matmul(
                        pu_[:], ixfm[:, k * TOK + tb * 128: k * TOK + (tb + 1) * 128],
                        wqu[qp][:, k * 512:(k + 1) * 512], start=(k == 0), stop=(k == KC - 1)),
                        "mm1" if k == KC - 1 else None,
                        mark=("C1mmP", idx) if k == KC - 1 else None)
                if tb == NB - 1:
                    marks[("C1mmN", n)] = ("mm1", cnt["mm1"])
                waitm("scalar", ("C1mmP", idx))
                if idx >= 2:
                    waitm("scalar", ("C1hsv", idx - 2))
                opE("scalar", lambda e, tp=tp, tb=tb, pg_=pg_: e.activation(
                    sgate[tp][:], pg_[:], AF.Silu, scale=scg[:, tb:tb + 1]), mark=("C1sil", idx))
                waitm("vector", ("C1mmP", idx))
                waitm("vector", ("C1sil", idx))
                if idx >= 2:
                    waitm("vector", ("C1hw", idx - 2))
                opE("vector", lambda e, tp=tp, tb=tb, pu_=pu_: e.scalar_tensor_tensor(
                    hst[tp][:], pu_[:], scu[:, tb:tb + 1], sgate[tp][:], ALU.mult, ALU.mult),
                    mark=("C1hsv", idx))
                waitm("gpsimd", ("C1hsv", idx))
                dma("gpsimd", h_a[tb * 128:(tb + 1) * 128, n * 512:(n + 1) * 512],
                    hst[tp][:], f"hw{tp}", mark=("C1hw", idx))
        if barrier():
            return

        waitv("gpsimd", "hw0", cnt["hw0"])
        waitv("gpsimd", "hw1", cnt["hw1"])
        for tb in range(NB):
            hp = tb % 2
            if tb >= 2:
                waitm("sync", ("C2strips", tb - 2))
            dma("sync", hfull[hp][:], h_a[tb * 128:(tb + 1) * 128, :], f"hr{hp}", mark=("C2hr", tb))
            waitm("scalar", ("C2hr", tb))
            opE("scalar", lambda e, hp=hp, tb=tb: e.activation(
                sqh_scr[:], hfull[hp][:, :H // 2], AF.Square, accum_out=hsq1[:, tb:tb + 1]))
            opE("scalar", lambda e, hp=hp, tb=tb: e.activation(
                sqh_scr[:], hfull[hp][:, H // 2:], AF.Square, accum_out=hsq2[:, tb:tb + 1]),
                mark=("C2sq", tb))
            waitm("vector", ("C2hr", tb))
            opE("vector", lambda e, hp=hp, tb=tb: e.reduce_max(
                maxh[:, tb:tb + 1], hfull[hp][:], AX.X, apply_absolute_value=True))
            ts_ = slice(tb, tb + 1)
            waitm("vector", ("C2sq", tb))
            opE("vector", lambda e, ts_=ts_: e.tensor_tensor(
                sumsqh[:, ts_], hsq1[:, ts_], hsq2[:, ts_], ALU.add))
            opE("vector", lambda e, ts_=ts_: e.tensor_scalar(
                tmpa[:, ts_], sumsqh[:, ts_], 1.0 / H, None, ALU.mult), mark=("C2e", tb))
            waitm("scalar", ("C2e", tb))
            opE("scalar", lambda e, ts_=ts_: e.activation(
                tmpb[:, ts_], tmpa[:, ts_], AF.Sqrt, bias=c6[:]), mark=("C2sqh", tb))
            waitm("vector", ("C2sqh", tb))
            opE("vector", lambda e, ts_=ts_: e.reciprocal(tmpc[:, ts_], tmpb[:, ts_]))
            opE("vector", lambda e, ts_=ts_: e.tensor_tensor(tmpa[:, ts_], maxh[:, ts_], tmpc[:, ts_], ALU.mult))
            opE("vector", lambda e, ts_=ts_: e.tensor_scalar(tmpa[:, ts_], tmpa[:, ts_], 1e-5, None, ALU.max))
            opE("vector", lambda e, ts_=ts_: e.tensor_scalar(sinvh[:, ts_], tmpa[:, ts_], 1.0 / 127.0, None, ALU.mult))
            opE("vector", lambda e, ts_=ts_: e.reciprocal(tmpb[:, ts_], tmpa[:, ts_]))
            opE("vector", lambda e, ts_=ts_: e.tensor_tensor(tmpc[:, ts_], tmpc[:, ts_], tmpb[:, ts_], ALU.mult))
            opE("vector", lambda e, ts_=ts_: e.tensor_scalar(alphah[:, ts_], tmpc[:, ts_], 127.0, None, ALU.mult),
                mark=("C2a", tb))
            if tb >= 1 and "c2ihw" not in KSKIP:
                waitm("scalar", ("C2ihw", tb - 1))
            for st in (range(H // 512) if "c2st" not in KSKIP else []):
                bp = st % 2
                opE("vector", lambda e, hp=hp, st=st, tb=tb: e.tensor_scalar(
                    ybufH[:], hfull[hp][:, st * 512:(st + 1) * 512], alphah[:, tb:tb + 1], None, ALU.mult))
                opE("vector", lambda e: e.tensor_scalar(i8h[:], ybufH[:], 1.0, None, ALU.mult))
                opE("vector", lambda e: e.tensor_tensor(rbufH[:], ybufH[:], i8h[:], ALU.subtract))
                opE("vector", lambda e: e.tensor_scalar(cbufH[:], rbufH[:], 0.5, None, ALU.is_gt))
                opE("vector", lambda e: e.scalar_tensor_tensor(
                    ybufH[:], rbufH[:], -0.5, cbufH[:], ALU.is_lt, ALU.subtract))
                if st >= 2 and "c2tr" not in KSKIP:
                    waitm("vector", ("C2ev", tb, st - 2))
                opE("vector", lambda e, bp=bp: e.tensor_tensor(
                    ibfh[bp][:], i8h[:], ybufH[:], ALU.subtract),
                    mark=("C2ib", tb, st))
                if "c2tr" not in KSKIP:
                    pairsC = [(ibfh[bp][:, c4 * 128:(c4 + 1) * 128],
                               ihstage[:, (st * 4 + c4) * 128:(st * 4 + c4 + 1) * 128])
                              for c4 in range(4)]
                    marks[("C2ev", tb, st)] = trstream(
                        "C", "tpC", pairsC, ident[:], [ptrA, ptrB], 8, ("C2ib", tb, st),
                        ev="scalar")
            marks[("C2strips", tb)] = ("vch", cnt["vch"])
            if "c2st" not in KSKIP and "c2tr" not in KSKIP:
                marks[("C2evall", tb)] = marks[("C2ev", tb, H // 512 - 1)]
            elif "c2tr" in KSKIP:
                marks[("C2evall", tb)] = ("vch", cnt["vch"])
            if "c2ihw" not in KSKIP:
                waitm("gpsimd", ("C2evall", tb) if "c2st" not in KSKIP else ("C2a", tb))
                for hkg in range(HKG):
                    dma("gpsimd", ih_a[(hkg * NB + tb) * 128:(hkg * NB + tb + 1) * 128, :],
                        ihstage[:, hkg * 1024:(hkg + 1) * 1024], f"ihw{tb % 2}")
            marks[("C2ihw", tb)] = (f"ihw{tb % 2}", cnt[f"ihw{tb % 2}"])
        if barrier():
            return

        waitv("gpsimd", "ihw0", cnt["ihw0"])
        waitv("gpsimd", "ihw1", cnt["ihw1"])
        waitm("sync", ("AGw",))
        for hkg in range(HKG):
            wp = hkg % 2
            if hkg >= 2:
                waitm("sync", ("C3mm", hkg - 2))
            for hk2 in range(8):
                dma("sync", wdq[wp][:, hk2 * D:(hk2 + 1) * D],
                    wdqa[(hkg * 8 + hk2) * 128:(hkg * 8 + hk2 + 1) * 128, :],
                    f"wdl{wp}", mark=("C3q", hkg, hk2))
            for tb in range(NB):
                idx = hkg * NB + tb
                ip = idx % 2
                if idx >= 2:
                    waitm("scalar", ("C3mmI", idx - 2))
                dma("scalar", ihsb[ip][:], ih_a[(hkg * NB + tb) * 128:(hkg * NB + tb + 1) * 128, :],
                    f"ihr{ip}", mark=("C3ihr", idx))
                for ds in range(NDS):
                    yp = (idx * NDS + ds) % 2
                    pys = [psA[:], psB[:]] if yp == 0 else [psC[:], psD[:]]
                    waitm("tensor", ("C3ihr", idx))
                    waitm("tensor", ("C3q", hkg, 7))
                    if (idx * NDS + ds) >= 2:
                        waitm("tensor", ("C3yac", idx * NDS + ds - 2))
                    nhalf = 2 if D >= 1024 else 1
                    for k2 in range(8):
                        for hf in range(nhalf):
                            c0 = ds * 1024 + hf * DW
                            opT(lambda e, ip=ip, k2=k2, hf=hf, wp=wp, c0=c0, pys=pys: e.matmul(
                                pys[hf][:, :DW], ihsb[ip][:, k2 * 128:(k2 + 1) * 128],
                                wdq[wp][:, k2 * D + c0: k2 * D + c0 + DW],
                                start=(k2 == 0), stop=(k2 == 7)),
                                "mm3" if (k2 == 7 and hf == nhalf - 1) else None,
                                mark=("C3mmP", idx, ds) if (k2 == 7 and hf == nhalf - 1) else None)
                    waitm("vector", ("C3mmP", idx, ds))
                    for hf in range(nhalf):
                        c0 = tb * D + ds * 1024 + hf * DW
                        if hkg == 0:
                            opE("vector", lambda e, c0=c0, hf=hf, pys=pys: e.tensor_copy(
                                yacc[:, c0:c0 + DW], pys[hf][:, :DW]),
                                mark=("C3yac", idx * NDS + ds) if hf == nhalf - 1 else None)
                        else:
                            opE("vector", lambda e, c0=c0, hf=hf, pys=pys: e.tensor_tensor(
                                yacc[:, c0:c0 + DW], yacc[:, c0:c0 + DW], pys[hf][:, :DW], ALU.add),
                                mark=("C3yac", idx * NDS + ds) if hf == nhalf - 1 else None)
                marks[("C3mmI", idx)] = ("vch", cnt["vch"])
            marks[("C3mm", hkg)] = ("mm3", cnt["mm3"])
        if barrier():
            return

        waitv("gpsimd", "sw0", cnt["sw0"])
        waitv("gpsimd", "sw1", cnt["sw1"])
        opE("vector", lambda e: e.tensor_scalar(scd[:, nbs], sinvh[:, nbs], wclip[:, 2:3], -1.0, ALU.mult, ALU.mult),
            mark=("Dscd",))
        for tb in range(NB):
            p = tb % 2
            if tb >= 2:
                waitm("sync", ("Dsg", tb - 2))
            dma("sync", s_tm[p][:], s_a[tb * 128:(tb + 1) * 128, :], f"sr{p}")
            dma("sync", p_tm[p][:], p_a[tb * 128:(tb + 1) * 128, :], f"sr{p}",
                mark=("Dsr", tb))
            if tb >= 2:
                waitm("scalar", ("Dob", tb - 2))
            dma("scalar", xod[p][:], xo[tb * 128:(tb + 1) * 128, :], f"xol{p}", mark=("Dxo", tb))
            waitm("scalar", ("Dscd",))
            if tb >= 1:
                waitm("scalar", ("Dsg", tb - 1))
            opE("scalar", lambda e, tb=tb: e.activation(
                gfin[:], yacc[:, tb * D:(tb + 1) * D], AF.Silu, scale=scd[:, tb:tb + 1]),
                mark=("Dgf", tb))
            waitm("vector", ("Dgf", tb))
            waitm("vector", ("Dsr", tb))
            opE("vector", lambda e, p=p: e.tensor_tensor(sgb[:], p_tm[p][:], Hb[:, 0:E], ALU.mult))
            opE("vector", lambda e, p=p: e.tensor_tensor(s_tm[p][:], s_tm[p][:], sgb[:], ALU.add))
            opE("vector", lambda e, p=p: e.tensor_tensor(sgb[:], s_tm[p][:], gfin[:], ALU.mult),
                mark=("Dsg", tb))
            waitm("vector", ("Dxo", tb))
            if tb >= 2:
                waitm("vector", ("Dout", tb - 2))
            opE("vector", lambda e, p=p: e.tensor_tensor(outsb[p][:], sgb[:], xod[p][:], ALU.add),
                mark=("Dob", tb))
            waitm("gpsimd", ("Dob", tb))
            dma("gpsimd", out[tb * 128:(tb + 1) * 128, :], outsb[p][:], f"outw{p}", mark=("Dout", tb))
        waitv("gpsimd", "outw0", cnt["outw0"])
        waitv("gpsimd", "outw1", cnt["outw1"])

    from contextlib import ExitStack as ES2
    with nc.Block() as block:
        @block.sync
        def _(e):
            describe("sync", e, {})

        @block.vector
        def _(e):
            describe("vector", e, {})

        @block.scalar
        def _(e):
            describe("scalar", e, {})

        @block.gpsimd
        def _(e):
            describe("gpsimd", e, {})

        @block.tensor
        def _(e):
            describe("tensor", e, {})

    stack.close()
    return nc


_NC_CACHE = {}


def _get_nc(S, D, H):
    key = (S, D, H)
    if key not in _NC_CACHE:
        _NC_CACHE[key] = build_nc(S, D, H)
    return _NC_CACHE[key]


def _pair_permute_wt(W, E):
    WT = np.ascontiguousarray(W.T)
    cols = []
    for cc in range(E // 128):
        cols.append(WT[:, cc * 128:(cc + 1) * 128])
        cols.append(WT[:, E + cc * 128: E + (cc + 1) * 128])
    return np.ascontiguousarray(np.concatenate(cols, axis=1))


def run(x, W, wgate, wup, wdown):
    B, S, D = x.shape
    H = wgate.shape[0]
    E = D
    nc = _get_nc(S, D, H)
    x = np.ascontiguousarray(np.asarray(x, np.float32))
    wtp = _pair_permute_wt(np.asarray(W, np.float32), E)
    wgt = np.ascontiguousarray(np.asarray(wgate, np.float32).T)
    wut = np.ascontiguousarray(np.asarray(wup, np.float32).T)
    wdt = np.ascontiguousarray(np.asarray(wdown, np.float32).T)
    TOK = S // 4
    in_maps = []
    for cid in range(8):
        b, j = cid // 4, cid % 4
        m = np.zeros(16, np.float32)
        for r in range(8):
            m[r] = 1.0 if (r // 4 == b and r % 4 < j) else 0.0
            m[8 + r] = 1.0 - m[r]
        shw = wtp.shape[0] // 8
        shd = wdt.shape[0] // 8
        in_maps.append({
            "xown": np.ascontiguousarray(x[b, j * TOK:(j + 1) * TOK]),
            "wtsh": np.ascontiguousarray(wtp[cid * shw:(cid + 1) * shw]),
            "wgsh": np.ascontiguousarray(wgt[cid * shw:(cid + 1) * shw]),
            "wush": np.ascontiguousarray(wut[cid * shw:(cid + 1) * shw]),
            "wdsh": np.ascontiguousarray(wdt[cid * shd:(cid + 1) * shd]),
            "cmask": np.ascontiguousarray(np.tile(m, (128, 1))),
        })
    res = run_bass_kernel_spmd(nc, in_maps, core_ids=list(range(8)))
    outs = [np.asarray(res.results[c]["out"]) for c in range(8)]
    full = np.stack([np.concatenate(outs[0:4], axis=0),
                     np.concatenate(outs[4:8], axis=0)], axis=0)
    return full.astype(np.float32)


def kernel(L0_attn_x, L0_attn_W, L0_ffn_w_gate, L0_ffn_w_up, L0_ffn_w_down):
    return run(L0_attn_x, L0_attn_W, L0_ffn_w_gate, L0_ffn_w_up, L0_ffn_w_down)


# version 34
# speedup vs baseline: 1.1080x; 1.0057x over previous
import os
import numpy as np
import concourse.bass as bass
import concourse.mybir as mybir
from concourse.bass_utils import run_bass_kernel_spmd

F32 = mybir.dt.float32
BF16 = mybir.dt.bfloat16
I8 = mybir.dt.int8
I32 = mybir.dt.int32
AF = mybir.ActivationFunctionType
ALU = mybir.AluOpType
AX = mybir.AxisListType

ENGS = ("sync", "vector", "scalar", "gpsimd", "tensor")


def build_nc(S, D, H):
    E = D
    TOK = S // 4
    NB = TOK // 128
    KC = D // 128
    CC = E // 128
    NPG = CC // 2
    GT = min(512, S // 4)
    NG = (S // 4) // GT
    GB = GT // 128
    NSL = H // 512
    HK = H // 128
    HKG = HK // 8
    NDS = max(1, D // 1024)
    DW = min(512, D)
    WSF = 1024

    nc = bass.Bass("TRN2", target_bir_lowering=False)

    SHD_W = D // 8
    SHD_D = H // 8
    xo_d = nc.dram_tensor("xown", [TOK, D], F32, kind="ExternalInput")
    wtsh_d = nc.dram_tensor("wtsh", [SHD_W, 2 * E], F32, kind="ExternalInput")
    wgsh_d = nc.dram_tensor("wgsh", [SHD_W, H], F32, kind="ExternalInput")
    wush_d = nc.dram_tensor("wush", [SHD_W, H], F32, kind="ExternalInput")
    wdsh_d = nc.dram_tensor("wdsh", [SHD_D, D], F32, kind="ExternalInput")
    cm_d = nc.dram_tensor("cmask", [128, 16], F32, kind="ExternalInput")
    out_d = nc.dram_tensor("out", [TOK, D], F32, kind="ExternalOutput")
    wtbfsh_d = nc.dram_tensor("wtbfsh", [SHD_W, 2 * E], BF16, kind="Internal")
    wgqsh_d = nc.dram_tensor("wgqsh", [SHD_W, H], BF16, kind="Internal")
    wuqsh_d = nc.dram_tensor("wuqsh", [SHD_W, H], BF16, kind="Internal")
    wdqsh_d = nc.dram_tensor("wdqsh", [SHD_D, D], BF16, kind="Internal")
    wtbf_d = nc.dram_tensor("wtbf", [D, 2 * E], BF16, kind="Internal", addr_space="Shared")
    wgq_d = nc.dram_tensor("wgq", [D, H], BF16, kind="Internal", addr_space="Shared")
    wuq_d = nc.dram_tensor("wuq", [D, H], BF16, kind="Internal", addr_space="Shared")
    wdq_d = nc.dram_tensor("wdq_g", [H, D], BF16, kind="Internal", addr_space="Shared")
    s_d = nc.dram_tensor("s_scr", [TOK, E], F32, kind="Internal")
    p_d = nc.dram_tensor("p_scr", [TOK, E], F32, kind="Internal")
    ccin_d = nc.dram_tensor("cc_in", [128, 32], F32, kind="Internal")
    ccout_d = nc.dram_tensor("cc_out", [8 * 128, 32], F32, kind="Internal", addr_space="Shared")
    h_d = nc.dram_tensor("h_scr", [TOK, H], F32, kind="Internal")
    ih_d = nc.dram_tensor("ih_scr", [HKG * NB * 128, 8 * 128], BF16, kind="Internal")

    xo = xo_d.ap(); wtsh = wtsh_d.ap(); wgsh = wgsh_d.ap()
    wush = wush_d.ap(); wdsh = wdsh_d.ap(); cm = cm_d.ap(); out = out_d.ap()
    wtbf = wtbf_d.ap(); s_a = s_d.ap(); h_a = h_d.ap(); ih_a = ih_d.ap()
    p_a = p_d.ap(); wgq = wgq_d.ap(); wuq = wuq_d.ap(); wdqa = wdq_d.ap()
    wtbfsh = wtbfsh_d.ap(); wgqsh = wgqsh_d.ap(); wuqsh = wuqsh_d.ap()
    wdqsh = wdqsh_d.ap()

    def sb(name, shape, dt):
        return nc.alloc_sbuf_tensor(name, shape, dt)

    ident = sb("ident", [128, 128], BF16)
    identf = sb("identf", [128, 128], F32)
    identi = sb("identi", [128, 128], I32)
    ones_col = sb("ones_col", [128, 1], F32)
    ones_row = sb("ones_row", [1, 128], F32)
    c5 = sb("c5", [128, 1], F32)
    c6 = sb("c6", [128, 1], F32)
    cm_sb = sb("cm_sb", [128, 16], F32)
    Hcar = sb("Hcar", [128, 16], F32)
    Htr = sb("Htr", [16, 128], F32)
    Rrow = sb("Rrow", [1, 128], F32)
    cstg = sb("cstg", [128, 32], F32)
    ccg = sb("ccg", [128, 256], F32)
    tmpH = sb("tmpH", [128, 16], F32); tmpP = sb("tmpP", [128, 16], F32)
    tmpQ = sb("tmpQ", [128, 16], F32)
    carryP = sb("carryP", [128, 16], F32)
    STAT = lambda n: sb(n, [128, max(NB, 4)], F32)
    musum = STAT("musum"); sumsq = STAT("sumsq"); maxx = STAT("maxx"); minn = STAT("minn")
    mu = STAT("mu"); rho = STAT("rho"); rr = STAT("rr"); alpha = STAT("alpha")
    sinv = STAT("sinv"); tmpa = STAT("tmpa"); tmpb = STAT("tmpb"); tmpc = STAT("tmpc")
    sumsqh = STAT("sumsqh"); maxh = STAT("maxh"); alphah = STAT("alphah"); sinvh = STAT("sinvh")
    hsq1 = STAT("hsq1"); hsq2 = STAT("hsq2")
    scg = STAT("scg"); scu = STAT("scu"); scd = STAT("scd")
    carry = sb("carry", [128, 16], F32)
    wacc = sb("wacc", [128, 64], F32)
    wbar = sb("wbar", [128, 4], F32); wclip = sb("wclip", [128, 4], F32)
    winv = sb("winv", [128, 4], F32)
    wth = sb("wth", [128, 4], F32); wmth = sb("wmth", [128, 4], F32)
    Hb = sb("Hb", [128, max(E, 4 * 128)], F32)
    ixfm = sb("ixfm", [128, KC * TOK], BF16)
    yacc = sb("yacc", [128, NB * D], F32)

    ARENA_BYTES = 98 * 1024
    arena = nc.alloc_sbuf_tensor("arena", [128, ARENA_BYTES], I8)
    ar_ap = {F32: arena.bitcast(F32).ap(), BF16: arena.bitcast(BF16).ap(),
             I8: arena.bitcast(I8).ap()}
    ELSZ = {F32: 4, BF16: 2, I8: 1, I32: 4}

    def trans_alloc():
        p = [0]
        def t(name, shape, dt):
            at = p[0]
            w = shape[1] * ELSZ[dt]
            p[0] = at + ((w + 31) // 32) * 32
            assert p[0] <= ARENA_BYTES, (name, p[0])
            return ar_ap[dt][:, at // ELSZ[dt]: at // ELSZ[dt] + shape[1]]
        return t

    t = trans_alloc()
    xa = [t(f"xa{i}", [128, D], F32) for i in range(2)]
    sqscr = t("sqscr", [128, max(D, WSF)], BF16)
    i8a = t("i8a", [128, D], I8)
    ibf = [t(f"ibf{i}", [128, D], BF16) for i in range(2)]
    ybufA = t("ybufA", [128, D], F32)
    rbufA = t("rbufA", [128, D], F32)
    cbufA = t("cbufA", [128, D], BF16)
    wtc_f = [t(f"wtcf{i}", [128, min(2048, 2 * E)], F32) for i in range(2)]
    wtc_b = t("wtcb", [128, min(2048, 2 * E)], BF16)
    ws_f = [t(f"wsf{i}", [128, 2048], F32) for i in range(2)]
    endA = None

    t = trans_alloc()
    _xg0 = t("xg0", [128, D], F32)
    xg = [_xg0, _xg0]
    xn_tm = [t(f"xntm{i}", [128, D], BF16) for i in range(2)]
    xn_fm = t("xnfm", [128, KC * GT], BF16)
    wts = [t(f"wts{i}", [128, KC * 512], BF16) for i in range(2)]
    qf = [t(f"qf{i}", [128, 512], F32) for i in range(2)]
    qb = [t(f"qb{i}", [128, 512], BF16) for i in range(2)]
    c1q = t("c1q", [128, 512], BF16)
    gsb = [t(f"gsb{i}", [128, GT], F32) for i in range(2)]
    qsb = [t(f"qsb{i}", [128, GT], F32) for i in range(2)]
    hscan = [t(f"hscan{i}", [128, GT], F32) for i in range(2)]
    pscan = [t(f"pscan{i}", [128, GT], F32) for i in range(2)]
    s_stg = [t(f"sstg{i}", [128, GT], F32) for i in range(2)]
    p_stg = [t(f"pstg{i}", [128, GT], F32) for i in range(2)]

    t = trans_alloc()
    wqg = [t(f"wqg{i}", [128, KC * 512], BF16) for i in range(2)]
    wqu = [t(f"wqu{i}", [128, KC * 512], BF16) for i in range(2)]
    sgate = [t(f"sgate{i}", [128, 512], F32) for i in range(2)]
    hst = [t(f"hst{i}", [128, 512], F32) for i in range(2)]

    t = trans_alloc()
    hfull = [t(f"hfull{i}", [128, H], F32) for i in range(2)]
    ihstage = t("ihstage", [128, HK * 128], BF16)
    i8h = t("i8h", [128, 512], I8)
    ibfh = [t(f"ibfh{i}", [128, 512], BF16) for i in range(2)]
    sqh_scr = t("sqh_scr", [128, H // 2], BF16)
    ybufH = t("ybufH", [128, 512], F32)
    rbufH = t("rbufH", [128, 512], F32)
    cbufH = t("cbufH", [128, 512], BF16)

    t = trans_alloc()
    wdq = [t(f"wdq{i}", [128, 8 * D], BF16) for i in range(2)]
    ihsb = [t(f"ihsb{i}", [128, 8 * 128], BF16) for i in range(2)]

    t = trans_alloc()
    s_tm = [t(f"stm{i}", [128, E], F32) for i in range(2)]
    p_tm = [t(f"ptm{i}", [128, E], F32) for i in range(2)]
    gfin = t("gfin", [128, D], F32)
    sgb = t("sgb", [128, D], F32)
    xod = [t(f"xod{i}", [128, D], F32) for i in range(2)]
    outsb = [t(f"outsb{i}", [128, D], F32) for i in range(2)]

    psA = nc.alloc_psum_tensor("psA", [128, 512], F32)
    psB = nc.alloc_psum_tensor("psB", [128, 512], F32)
    psC = nc.alloc_psum_tensor("psC", [128, 512], F32)
    psD = nc.alloc_psum_tensor("psD", [128, 512], F32)
    ptrA = nc.alloc_psum_tensor("ptrA", [128, 1024], BF16)
    ptrB = nc.alloc_psum_tensor("ptrB", [128, 1024], BF16)
    psS = nc.alloc_psum_tensor("psS", [128, 512], F32)
    psS2 = nc.alloc_psum_tensor("psS2", [128, 512], F32)

    sem_names = [
        "ph", "vch", "sch", "gch", "cml",
        "xa0", "xa1", "xbl0", "xbl1", "wtl0", "wtl1", "wtw",
        "wtsl0", "wtsl1", "wsl0", "wsl1", "hw0", "hw1", "qbw0", "qbw1",
        "hr0", "hr1", "ihw0", "ihw1", "ihr0", "ihr1", "wdl0", "wdl1", "c1l0", "c1l1",
        "sr0", "sr1", "xol0", "xol1", "outw0", "outw1", "sw0", "sw1", "ccs", "ccc", "ccd",
        "tpA", "tpB", "tpC", "tpD", "mmB", "mm1", "mm3",
    ]
    sems = {}
    from contextlib import ExitStack
    stack = ExitStack()
    for n in sem_names:
        sems[n] = stack.enter_context(nc.semaphore(n))

    WSTW = 2048
    wstat_tiles = []
    wranges = []
    for m, (ap_, R, C) in enumerate([(wgsh, SHD_W, H), (wush, SHD_W, H), (wdsh, SHD_D, D)]):
        lo = len(wstat_tiles)
        for r in range(max(1, R // 128)):
            for c0 in range(0, C, WSTW):
                wstat_tiles.append((ap_, r * 128, c0, min(WSTW, C - c0), min(128, R)))
        wranges.append((lo, len(wstat_tiles)))
    wquant_tiles = []
    for m, (sap, dap, R, C) in enumerate([(wgsh, wgqsh, SHD_W, H), (wush, wuqsh, SHD_W, H),
                                          (wdsh, wdqsh, SHD_D, D)]):
        for r in range(max(1, R // 128)):
            for c0 in range(0, C, 512):
                wquant_tiles.append((sap, dap, r * 128, c0, min(512, C - c0),
                                     min(128, R), m))

    CH = {"vector": "vch", "scalar": "sch", "gpsimd": "gch"}

    PHLIM = int(os.environ.get("KPHL", "99"))
    KSKIP = set(os.environ.get("KSKIP", "").split(","))

    def describe(real, eng, ctx):
        cnt = {n: 0 for n in sem_names}
        marks = {}
        phase_no = [0]

        def me(e):
            return e == real

        def opE(e, fn, mark=None):
            s = CH[e]
            if me(e):
                if cnt[s] > 0:
                    eng.wait_ge(sems[s], cnt[s])
                fn(eng).then_inc(sems[s], 1)
            cnt[s] += 1
            if mark is not None:
                marks[mark] = (s, cnt[s])

        def opT(fn, sem=None, mark=None):
            if me("tensor"):
                ins = fn(eng)
                if sem is not None:
                    ins.then_inc(sems[sem], 1)
            if sem is not None:
                cnt[sem] += 1
                if mark is not None:
                    marks[mark] = (sem, cnt[sem])

        def dma(e, dst, src, sem, mark=None):
            if me(e):
                eng.dma_start(dst, src).then_inc(sems[sem], 16)
            cnt[sem] += 16
            if mark is not None:
                marks[mark] = (sem, cnt[sem])

        def waitm(e, mark):
            s, v = marks[mark]
            if me(e) and v > 0:
                eng.wait_ge(sems[s], v)

        def waitv(e, sem, v):
            if me(e) and v > 0:
                eng.wait_ge(sems[sem], v)

        def waitcur(e, sem):
            waitv(e, sem, cnt[sem])

        GP_DMA = ["cml", "xa0", "xa1", "xbl0", "xbl1", "wtw", "hw0", "hw1",
                  "ihw0", "ihw1",
                  "outw0", "outw1", "sw0", "sw1"]
        GP_DMA += ["qbw0", "qbw1"]
        SY_DMA = ["wtl0", "wtl1", "wsl0", "wsl1", "c1l0", "c1l1",
                  "wdl0", "wdl1", "sr0", "sr1", "hr0", "hr1"]
        SC_DMA = ["wtsl0", "wtsl1", "ihr0", "ihr1", "xol0", "xol1", "wsl0", "wsl1"]
        ENG_DMA = {"gpsimd": GP_DMA, "sync": SY_DMA, "scalar": SC_DMA}

        def finish_early():
            for tb in range(NB):
                dma("gpsimd", out[tb * 128:(tb + 1) * 128, :], xod[0][:], f"outw{tb % 2}")
            waitv("gpsimd", "outw0", cnt["outw0"])
            waitv("gpsimd", "outw1", cnt["outw1"])

        def barrier():
            cnt["ph"] += 5
            if real in ENGS:
                drains = ENG_DMA.get(real, [])
                for s_ in drains:
                    if cnt[s_] > 0:
                        eng.wait_ge(sems[s_], cnt[s_])
                if real in CH and cnt[CH[real]] > 0:
                    eng.wait_ge(sems[CH[real]], cnt[CH[real]])
                eng.sem_inc(sems["ph"], 1)
                eng.wait_ge(sems["ph"], cnt["ph"])
            phase_no[0] += 1
            if phase_no[0] >= PHLIM:
                finish_early()
                return True
            return False

        tstates = {}

        def trstream(tag, tpsem, pairs, ident_ap, banks, gsz, src_mark, ev="vector"):
            st = tstates.setdefault(tag, {"grp": 0, "bmark": {0: None, 1: None}})
            last_ev = None
            i = 0
            while i < len(pairs):
                chunk = pairs[i:i + gsz]
                b = st["grp"] % 2
                bank = banks[b]
                if st["bmark"][b] is not None:
                    waitm("tensor", st["bmark"][b])
                if src_mark is not None:
                    waitm("tensor", src_mark)
                for sl, (srcap, dstap) in enumerate(chunk):
                    opT(lambda e, srcap=srcap, bank=bank, sl=sl, ident_ap=ident_ap:
                        e.transpose(bank[:, sl * 128:(sl + 1) * 128], srcap, ident_ap),
                        tpsem, mark=(tag, "tp", st["grp"]) if sl == len(chunk) - 1 else None)
                waitm(ev, (tag, "tp", st["grp"]))
                for sl, (srcap, dstap) in enumerate(chunk):
                    if ev == "scalar":
                        opE(ev, lambda e, dstap=dstap, bank=bank, sl=sl:
                            e.activation(dstap, bank[:, sl * 128:(sl + 1) * 128], AF.Copy),
                            mark=(tag, "ev", st["grp"]) if sl == len(chunk) - 1 else None)
                    else:
                        opE(ev, lambda e, dstap=dstap, bank=bank, sl=sl:
                            e.tensor_copy(dstap, bank[:, sl * 128:(sl + 1) * 128]),
                            mark=(tag, "ev", st["grp"]) if sl == len(chunk) - 1 else None)
                st["bmark"][b] = (tag, "ev", st["grp"])
                last_ev = (tag, "ev", st["grp"])
                st["grp"] += 1
                i += gsz
            return marks[last_ev]

        opE("gpsimd", lambda e: e.iota(identi[:], [[1, 128]], base=0, channel_multiplier=-1))
        opE("gpsimd", lambda e: e.tensor_scalar(ident[:], identi[:], 0, None, ALU.is_equal))
        opE("gpsimd", lambda e: e.tensor_scalar(identf[:], identi[:], 0, None, ALU.is_equal))
        opE("gpsimd", lambda e: e.memset(ones_col[:], 1.0))
        opE("gpsimd", lambda e: e.memset(ones_row[:], 1.0))
        opE("gpsimd", lambda e: e.memset(carry[:], 0.0))
        opE("gpsimd", lambda e: e.memset(carryP[:], 1.0))
        opE("gpsimd", lambda e: e.memset(Hcar[:], 0.0))
        opE("gpsimd", lambda e: e.memset(cstg[:], 0.0))
        opE("gpsimd", lambda e: e.memset(wacc[:], 0.0))
        opE("gpsimd", lambda e: e.memset(c5[:], 1e-5))
        opE("gpsimd", lambda e: e.memset(c6[:], 1e-6))
        dma("gpsimd", cm_sb[:], cm, "cml")
        waitv("gpsimd", "cml", 16)
        if barrier():
            return

        for tb in range(NB):
            p = tb % 2
            if tb >= 2:
                waitm("gpsimd", ("Ava", tb - 2))
                waitm("gpsimd", ("Asa", tb - 2))
            dma("gpsimd", xa[p][:], xo[tb * 128:(tb + 1) * 128, :], f"xa{p}", mark=("Axa", tb))
            waitm("vector", ("Axa", tb))
            opE("vector", lambda e, p=p, tb=tb: e.reduce_sum(musum[:, tb:tb + 1], xa[p][:], AX.X))
            opE("vector", lambda e, p=p, tb=tb: e.reduce_max(maxx[:, tb:tb + 1], xa[p][:], AX.X))
            opE("vector", lambda e, p=p, tb=tb: e.tensor_reduce(
                minn[:, tb:tb + 1], xa[p][:], AX.X, ALU.min), mark=("Ava", tb))
            waitm("scalar", ("Axa", tb))
            opE("scalar", lambda e, p=p, tb=tb: e.activation(
                sqscr[:, :D], xa[p][:], AF.Square, accum_out=sumsq[:, tb:tb + 1]),
                mark=("Asa", tb))
        waitm("vector", ("Ava", NB - 1))
        waitm("vector", ("Asa", NB - 1))
        nbs = slice(0, NB)
        opE("vector", lambda e: e.tensor_scalar(mu[:, nbs], musum[:, nbs], 1.0 / D, None, ALU.mult))
        opE("vector", lambda e: e.tensor_scalar(tmpa[:, nbs], sumsq[:, nbs], 1.0 / D, None, ALU.mult))
        opE("vector", lambda e: e.tensor_tensor(tmpb[:, nbs], mu[:, nbs], mu[:, nbs], ALU.mult))
        opE("vector", lambda e: e.tensor_tensor(tmpa[:, nbs], tmpa[:, nbs], tmpb[:, nbs], ALU.subtract),
            mark=("Avar",))
        waitm("scalar", ("Avar",))
        opE("scalar", lambda e: e.activation(tmpb[:, nbs], tmpa[:, nbs], AF.Sqrt, bias=c5[:]),
            mark=("Asq1",))
        waitm("vector", ("Asq1",))
        opE("vector", lambda e: e.reciprocal(rho[:, nbs], tmpb[:, nbs]))
        opE("vector", lambda e: e.tensor_scalar(tmpb[:, nbs], tmpa[:, nbs], 1e-5, None, ALU.add))
        opE("vector", lambda e: e.reciprocal(tmpc[:, nbs], tmpb[:, nbs]))
        opE("vector", lambda e: e.tensor_tensor(tmpb[:, nbs], tmpa[:, nbs], tmpc[:, nbs], ALU.mult),
            mark=("Aq2",))
        waitm("scalar", ("Aq2",))
        opE("scalar", lambda e: e.activation(tmpc[:, nbs], tmpb[:, nbs], AF.Sqrt, bias=c6[:]),
            mark=("Asq2",))
        waitm("vector", ("Asq2",))
        opE("vector", lambda e: e.reciprocal(rr[:, nbs], tmpc[:, nbs]))
        opE("vector", lambda e: e.tensor_tensor(tmpa[:, nbs], maxx[:, nbs], mu[:, nbs], ALU.subtract))
        opE("vector", lambda e: e.tensor_tensor(tmpb[:, nbs], mu[:, nbs], minn[:, nbs], ALU.subtract))
        opE("vector", lambda e: e.tensor_tensor(tmpa[:, nbs], tmpa[:, nbs], tmpb[:, nbs], ALU.max))
        opE("vector", lambda e: e.tensor_tensor(tmpa[:, nbs], tmpa[:, nbs], rho[:, nbs], ALU.mult))
        opE("vector", lambda e: e.tensor_tensor(tmpa[:, nbs], tmpa[:, nbs], rr[:, nbs], ALU.mult))
        opE("vector", lambda e: e.tensor_scalar(tmpa[:, nbs], tmpa[:, nbs], 1e-5, None, ALU.max))
        opE("vector", lambda e: e.tensor_scalar(sinv[:, nbs], tmpa[:, nbs], 1.0 / 127.0, None, ALU.mult))
        opE("vector", lambda e: e.reciprocal(tmpb[:, nbs], tmpa[:, nbs]))
        opE("vector", lambda e: e.tensor_tensor(tmpc[:, nbs], rho[:, nbs], rr[:, nbs], ALU.mult))
        opE("vector", lambda e: e.tensor_tensor(tmpc[:, nbs], tmpc[:, nbs], tmpb[:, nbs], ALU.mult))
        opE("vector", lambda e: e.tensor_scalar(alpha[:, nbs], tmpc[:, nbs], 127.0, None, ALU.mult),
            mark=("Astat",))

        wtcW = min(2048, 2 * E)
        wtcR = min(128, SHD_W)
        wtc_tiles = [(k, c0) for k in range(max(1, SHD_W // 128))
                     for c0 in range(0, 2 * E, wtcW)]
        wtc_per = (len(wtc_tiles) + NB - 1) // NB
        ws_per = (len(wstat_tiles) + NB - 1) // NB
        for tb in range(NB):
            p = tb % 2
            waitm("gpsimd", ("Astat",))
            if tb >= 2:
                waitm("gpsimd", ("Aq", tb - 2))
            dma("gpsimd", xa[p][:], xo[tb * 128:(tb + 1) * 128, :], f"xa{p}", mark=("Axa2", tb))
            waitm("vector", ("Axa2", tb))
            opE("vector", lambda e, p=p, tb=tb: e.tensor_scalar(
                ybufA[:], xa[p][:], mu[:, tb:tb + 1], alpha[:, tb:tb + 1], ALU.subtract, ALU.mult))
            opE("vector", lambda e: e.tensor_scalar(i8a[:], ybufA[:], 1.0, None, ALU.mult))
            opE("vector", lambda e: e.tensor_tensor(rbufA[:], ybufA[:], i8a[:], ALU.subtract))
            opE("vector", lambda e: e.tensor_scalar(cbufA[:], rbufA[:], 0.5, None, ALU.is_gt))
            opE("vector", lambda e: e.scalar_tensor_tensor(
                ybufA[:], rbufA[:], -0.5, cbufA[:], ALU.is_lt, ALU.subtract))
            if tb >= 2:
                waitm("vector", ("AevA", tb - 2))
            opE("vector", lambda e, p=p: e.tensor_tensor(
                ibf[p][:], i8a[:], ybufA[:], ALU.subtract), mark=("Aq", tb))
            pairsA = [(ibf[p][:, c * 128:(c + 1) * 128],
                       ixfm[:, c * TOK + tb * 128: c * TOK + (tb + 1) * 128])
                      for c in range(KC)]
            marks[("AevA", tb)] = trstream(
                "A", "tpA", pairsA, ident[:], [ptrA, ptrB], 8, ("Aq", tb))
            for i in range(tb * wtc_per, min((tb + 1) * wtc_per, len(wtc_tiles))):
                k, c0 = wtc_tiles[i]
                pw = i % 2
                if i >= 2:
                    waitm("sync", ("Awtv", i - 2))
                dma("sync", wtc_f[pw][:wtcR, :wtcW],
                    wtsh[k * 128:k * 128 + wtcR, c0:c0 + wtcW],
                    f"wtl{pw}", mark=("Awtl", i))
                waitm("vector", ("Awtl", i))
                if i >= 1:
                    waitm("vector", ("Awtw", i - 1))
                opE("vector", lambda e, pw=pw: e.tensor_copy(
                    wtc_b[:wtcR, :wtcW], wtc_f[pw][:wtcR, :wtcW]), mark=("Awtv", i))
                waitm("gpsimd", ("Awtv", i))
                dma("gpsimd", wtbfsh[k * 128:k * 128 + wtcR, c0:c0 + wtcW],
                    wtc_b[:wtcR, :wtcW], "wtw", mark=("Awtw", i))
            for i in range(tb * ws_per, min((tb + 1) * ws_per, len(wstat_tiles))):
                ap_, r0, c0, fd, nr = wstat_tiles[i]
                pw = i % 2
                qe = "sync" if pw == 0 else "scalar"
                if i >= 2:
                    waitm(qe, ("Wab", i - 2))
                dma(qe, ws_f[pw][:nr, :fd], ap_[r0:r0 + nr, c0:c0 + fd],
                    f"wsl{pw}", mark=("Wsl", i))
                waitm("scalar", ("Wsl", i))
                opE("scalar", lambda e, pw=pw, fd=fd, nr=nr, i=i: e.activation(
                    sqscr[:nr, :fd], ws_f[pw][:nr, :fd], AF.Abs, accum_out=wacc[:nr, i:i + 1]),
                    mark=("Wab", i))
        if barrier():
            return

        waitv("gpsimd", "wtw", cnt["wtw"])
        if me("gpsimd"):
            eng.collective_compute(
                "AllGather", ALU.bypass, replica_groups=[list(range(8))],
                ins=[wtbfsh_d.ap().opt()], outs=[wtbf_d.ap().opt()]).then_inc(sems["ccs"], 1)
        cnt["ccs"] += 1
        marks[("AGwt",)] = ("ccs", cnt["ccs"])
        opE("gpsimd", lambda e: e.memset(cstg[:], 0.0), mark=("Cst0",))
        for m in range(3):
            lo, hi = wranges[m]
            waitm("vector", ("Wab", hi - 1))
            if m == 0:
                waitm("vector", ("Cst0",))
            opE("vector", lambda e, m=m, lo=lo, hi=hi: e.reduce_sum(
                cstg[:, m:m + 1], wacc[:, lo:hi], AX.X), mark=("Wsum", m))
        waitm("gpsimd", ("Wsum", 2))
        dma("gpsimd", ccin_d.ap(), cstg[:], "ccd")
        waitv("gpsimd", "ccd", cnt["ccd"])
        if me("gpsimd"):
            eng.collective_compute(
                "AllGather", ALU.bypass, replica_groups=[list(range(8))],
                ins=[ccin_d.ap().opt()], outs=[ccout_d.ap().opt()]).then_inc(sems["ccs"], 1)
        cnt["ccs"] += 1
        waitv("gpsimd", "ccs", cnt["ccs"])
        for r in range(8):
            dma("gpsimd", ccg[:, r * 32:(r + 1) * 32],
                ccout_d.ap()[r * 128:(r + 1) * 128, :], "ccd",
                mark=("WSg",) if r == 7 else None)
        waitm("vector", ("WSg",))
        opE("vector", lambda e: e.tensor_copy(wbar[:, 0:3], ccg[:, 0:3]))
        for r in range(1, 8):
            opE("vector", lambda e, r=r: e.tensor_tensor(
                wbar[:, 0:3], wbar[:, 0:3], ccg[:, r * 32: r * 32 + 3], ALU.add),
                mark=("WSsum",) if r == 7 else None)
        waitm("tensor", ("WSsum",))
        for m in range(3):
            opT(lambda e, m=m: e.matmul(
                psS[0:1, m:m + 1], ones_col[:], wbar[:, m:m + 1], start=True, stop=True),
                "mm1", mark=("Wmm1", m))
        waitm("vector", ("Wmm1", 2))
        for m in range(3):
            opE("vector", lambda e, m=m: e.tensor_copy(wclip[0:1, m:m + 1], psS[0:1, m:m + 1]),
                mark=("Wcp", m))
        waitm("tensor", ("Wcp", 2))
        for m in range(3):
            opT(lambda e, m=m: e.matmul(
                psS[:, 8 + m:9 + m], ones_row[:], wclip[0:1, m:m + 1], start=True, stop=True),
                "mm1", mark=("Wmm2", m))
        waitm("vector", ("Wmm2", 2))
        opE("vector", lambda e: e.tensor_copy(wbar[:, 0:3], psS[:, 8:11]))
        opE("vector", lambda e: e.tensor_scalar(wbar[:, 0:3], wbar[:, 0:3], 1.0 / (D * H), None, ALU.mult))
        opE("vector", lambda e: e.tensor_scalar(wclip[:, 0:3], wbar[:, 0:3], 1e-5, None, ALU.max))
        opE("vector", lambda e: e.tensor_scalar(wth[:, 0:3], wclip[:, 0:3], 0.5, None, ALU.mult))
        opE("vector", lambda e: e.tensor_scalar(wmth[:, 0:3], wclip[:, 0:3], -0.5, None, ALU.mult))
        opE("vector", lambda e: e.tensor_scalar(scg[:, nbs], sinv[:, nbs], wclip[:, 0:1], -1.0, ALU.mult, ALU.mult))
        opE("vector", lambda e: e.tensor_scalar(scu[:, nbs], sinv[:, nbs], wclip[:, 1:2], -1.0, ALU.mult, ALU.mult),
            mark=("Wready",))

        qt_slots = max(1, (NG * NPG * 2) // 5)
        qt_per = (len(wquant_tiles) + qt_slots - 1) // qt_slots
        qt_next = [0]
        N_G = sum(1 for tle in wquant_tiles if tle[6] < 1)
        N_GU = sum(1 for tle in wquant_tiles if tle[6] < 2)
        NPALL = NG * NPG
        P_G = NPALL // 2
        P_U = (NPALL * 2) // 3

        def fire_ags(pairs, mk):
            waitv("gpsimd", "qbw0", cnt["qbw0"])
            waitv("gpsimd", "qbw1", cnt["qbw1"])
            for (insh, outfull) in pairs:
                if me("gpsimd"):
                    eng.collective_compute(
                        "AllGather", ALU.bypass, replica_groups=[list(range(8))],
                        ins=[insh.ap().opt()], outs=[outfull.ap().opt()]).then_inc(sems["ccs"], 1)
                cnt["ccs"] += 1
            marks[mk] = ("ccs", cnt["ccs"])

        def maybe_fire(p):
            if qt_next[0] >= N_G and p >= P_G and ("AGg",) not in marks:
                fire_ags([(wgqsh_d, wgq_d)], ("AGg",))
            if qt_next[0] >= N_GU and p >= P_U and ("AGu",) not in marks:
                fire_ags([(wuqsh_d, wuq_d)], ("AGu",))

        def quant_slot():
            hi = min(qt_next[0] + qt_per, len(wquant_tiles))
            for i in range(qt_next[0], hi):
                sap, dap, r0, c0, w, nr, m = wquant_tiles[i]
                pw = i % 2
                if i >= 2:
                    waitm("sync", ("Qq", i - 2))
                dma("sync", qf[pw][:nr, :w], sap[r0:r0 + nr, c0:c0 + w],
                    f"wsl{pw}", mark=("Ql", i))
                waitm("vector", ("Ql", i))
                if i >= 2:
                    waitm("vector", ("Qw", i - 2))
                opE("vector", lambda e, pw=pw, nr=nr, w=w, m=m: e.tensor_scalar(
                    c1q[:nr, :w], qf[pw][:nr, :w], wth[:nr, m:m + 1], None, ALU.is_gt))
                opE("vector", lambda e, pw=pw, nr=nr, w=w, m=m: e.scalar_tensor_tensor(
                    qb[pw][:nr, :w], qf[pw][:nr, :w], wmth[:nr, m:m + 1], c1q[:nr, :w],
                    ALU.is_lt, ALU.subtract), mark=("Qq", i))
                waitm("gpsimd", ("Qq", i))
                dma("gpsimd", dap[r0:r0 + nr, c0:c0 + w], qb[pw][:nr, :w], f"qbw{pw}",
                    mark=("Qw", i))
            qt_next[0] = hi

        waitm("scalar", ("AGwt",))
        waitm("sync", ("AGwt",))
        for g in range(NG):
            for tbp in range(GB):
                p = tbp % 2
                bidx = g * GB + tbp
                r0 = g * GT + tbp * 128
                prev_same_par = [t2 for t2 in range(GB) if t2 % 2 == tbp % 2][-1]
                if tbp >= 1:
                    waitm("gpsimd", ("Bxn", g, tbp - 1))
                elif g > 0:
                    waitm("gpsimd", ("Bxn", g - 1, GB - 1))
                dma("gpsimd", xg[p][:], xo[r0:r0 + 128, :], f"xbl{p}", mark=("Bxb2", g, tbp))
                waitm("vector", ("Bxb2", g, tbp))
                if g > 0 or tbp >= 2:
                    pm = (("BevX", g, tbp - 2) if tbp >= 2
                          else ("BevX", g - 1, prev_same_par))
                    waitm("vector", pm)
                opE("vector", lambda e, p=p, bidx=bidx: e.tensor_scalar(
                    xn_tm[p][:], xg[p][:], mu[:, bidx:bidx + 1], rho[:, bidx:bidx + 1],
                    ALU.subtract, ALU.mult), mark=("Bxn", g, tbp))
                if g > 0 and tbp == 0:
                    waitm("vector", ("BmmAll", g - 1))
                pairsB = [(xn_tm[p][:, c * 128:(c + 1) * 128],
                           xn_fm[:, c * GT + tbp * 128: c * GT + (tbp + 1) * 128])
                          for c in range(KC)]
                marks[("BevX", g, tbp)] = trstream(
                    "B", "tpB", pairsB, ident[:], [ptrA, ptrB], 8, ("Bxn", g, tbp))
            marks[("BevG", g)] = marks[("BevX", g, GB - 1)]
            for pg in range(NPG):
                sp = pg % 2
                qeng = "sync" if (g * NPG + pg) >= max(0, NPALL - 8) else "scalar"
                if pg >= 2 or g > 0:
                    pm = (("BmmPG", g, pg - 2) if pg >= 2
                          else ("BmmPG", g - 1, (pg - 2) % NPG))
                    waitm(qeng, pm)
                for k in range(KC):
                    dma(qeng, wts[sp][:, k * 512:(k + 1) * 512],
                        wtbf[k * 128:(k + 1) * 128, pg * 512:(pg + 1) * 512], f"wtsl{sp}")
                marks[("Bwtsl", g, pg)] = (f"wtsl{sp}", cnt[f"wtsl{sp}"])
                for pi in range(2):
                    cc = pg * 2 + pi
                    cp = cc % 2
                    pq = psA if cp == 0 else psC
                    pgm = psB if cp == 0 else psD
                    waitm("tensor", ("Bwtsl", g, pg))
                    waitm("tensor", ("BevG", g))
                    gidx = g * CC + cc
                    if gidx >= 2:
                        waitm("tensor", ("Bqv", gidx - 2))
                        waitm("tensor", ("Bsg", gidx - 2))
                    for k in range(KC):
                        opT(lambda e, k=k, sp=sp, pi=pi, pq=pq: e.matmul(
                            pq[:, :GT], wts[sp][:, k * 512 + pi * 256: k * 512 + pi * 256 + 128],
                            xn_fm[:, k * GT:(k + 1) * GT],
                            start=(k == 0), stop=(k == KC - 1)))
                        opT(lambda e, k=k, sp=sp, pi=pi, pgm=pgm: e.matmul(
                            pgm[:, :GT], wts[sp][:, k * 512 + pi * 256 + 128: k * 512 + pi * 256 + 256],
                            xn_fm[:, k * GT:(k + 1) * GT],
                            start=(k == 0), stop=(k == KC - 1)),
                            "mmB" if k == KC - 1 else None,
                            mark=("BmmP", gidx) if k == KC - 1 else None)
                    waitm("scalar", ("BmmP", gidx))
                    if gidx >= 2:
                        waitm("scalar", ("Bscn", gidx - 2))
                    opE("scalar", lambda e, cp=cp, pgm=pgm: e.activation(
                        gsb[cp][:], pgm[:, :GT], AF.Sigmoid), mark=("Bsg", gidx))
                    waitm("vector", ("BmmP", gidx))
                    if gidx >= 2:
                        waitm("vector", ("Bscn", gidx - 2))
                    opE("vector", lambda e, cp=cp, pq=pq: e.tensor_scalar(
                        qsb[cp][:], pq[:, :GT], 1e-20, None, ALU.max), mark=("Bqv", gidx))
                    waitm("vector", ("Bsg", gidx))
                    if gidx >= 2:
                        waitm("vector", ("BSev", gidx - 2))
                    opE("vector", lambda e, cp=cp, cc=cc: e.tensor_tensor_scan(
                        hscan[cp][:], gsb[cp][:], qsb[cp][:], carry[:, cc:cc + 1],
                        ALU.mult, ALU.add))
                    opE("vector", lambda e, cp=cp, cc=cc: e.tensor_copy(
                        carry[:, cc:cc + 1], hscan[cp][:, GT - 1:GT]))
                    opE("vector", lambda e, cp=cp, cc=cc: e.tensor_tensor_scan(
                        pscan[cp][:], gsb[cp][:], gsb[cp][:], carryP[:, cc:cc + 1],
                        ALU.mult, ALU.bypass))
                    opE("vector", lambda e, cp=cp, cc=cc: e.tensor_copy(
                        carryP[:, cc:cc + 1], pscan[cp][:, GT - 1:GT]), mark=("Bscn", gidx))
                    if gidx >= 2:
                        waitm("vector", ("Bsw", gidx - 2))
                    trstream("BS", "tpD",
                             [(hscan[cp][:, ti * 128:(ti + 1) * 128],
                               s_stg[cp][:, ti * 128:(ti + 1) * 128]) for ti in range(GB)],
                             identf[:], [psS, psS2], 4, ("Bscn", gidx))
                    marks[("BSev", gidx)] = trstream(
                        "BS", "tpD",
                        [(pscan[cp][:, ti * 128:(ti + 1) * 128],
                          p_stg[cp][:, ti * 128:(ti + 1) * 128]) for ti in range(GB)],
                        identf[:], [psS, psS2], 4, ("Bscn", gidx))
                    waitm("gpsimd", ("BSev", gidx))
                    for ti in range(GB):
                        dma("gpsimd",
                            s_a[g * GT + ti * 128: g * GT + (ti + 1) * 128,
                                cc * 128:(cc + 1) * 128],
                            s_stg[cp][:, ti * 128:(ti + 1) * 128], f"sw{cp}")
                        dma("gpsimd",
                            p_a[g * GT + ti * 128: g * GT + (ti + 1) * 128,
                                cc * 128:(cc + 1) * 128],
                            p_stg[cp][:, ti * 128:(ti + 1) * 128],
                            f"sw{cp}", mark=("Bsw", gidx) if ti == GB - 1 else None)
                marks[("BmmPG", g, pg)] = ("mmB", cnt["mmB"])
                quant_slot()
                maybe_fire(g * NPG + pg + 1)
            marks[("BmmAll", g)] = ("mmB", cnt["mmB"])
        quant_slot()
        maybe_fire(NPALL * 2)
        opE("vector", lambda e: e.tensor_copy(cstg[:, 0:CC], carry[:, 0:CC]))
        opE("vector", lambda e: e.tensor_copy(cstg[:, 16:16 + CC], carryP[:, 0:CC]),
            mark=("CCstg",))
        waitm("gpsimd", ("CCstg",))
        dma("gpsimd", ccin_d.ap(), cstg[:], "ccd")
        waitv("gpsimd", "ccd", cnt["ccd"])
        if me("gpsimd"):
            eng.collective_compute(
                "AllGather", ALU.bypass, replica_groups=[list(range(8))],
                ins=[ccin_d.ap().opt()], outs=[ccout_d.ap().opt()]).then_inc(sems["ccc"], 1)
        cnt["ccc"] += 1
        marks[("AGc",)] = ("ccc", cnt["ccc"])
        fire_ags([(wdqsh_d, wdq_d)], ("AGw",))
        if barrier():
            return

        waitm("gpsimd", ("AGc",))
        for r in range(8):
            dma("gpsimd", ccg[:, r * 32:(r + 1) * 32],
                ccout_d.ap()[r * 128:(r + 1) * 128, :], "ccd",
                mark=("CCg",) if r == 7 else None)
        waitm("vector", ("CCg",))
        for r in range(8):
            opE("vector", lambda e, r=r: e.tensor_scalar(
                tmpH[:, 0:CC], ccg[:, r * 32: r * 32 + CC], cm_sb[:, r:r + 1], None, ALU.mult))
            opE("vector", lambda e, r=r: e.tensor_scalar(
                tmpP[:, 0:CC], ccg[:, r * 32 + 16: r * 32 + 16 + CC],
                cm_sb[:, r:r + 1], cm_sb[:, 8 + r:9 + r], ALU.mult, ALU.add))
            opE("vector", lambda e: e.tensor_tensor(tmpQ[:, 0:CC], tmpP[:, 0:CC], Hcar[:, 0:CC], ALU.mult))
            opE("vector", lambda e: e.tensor_tensor(Hcar[:, 0:CC], tmpQ[:, 0:CC], tmpH[:, 0:CC], ALU.add),
                mark=("CCH",) if r == 7 else None)
        waitm("tensor", ("CCH",))
        opT(lambda e: e.transpose(psS[0:CC, 0:128], Hcar[:, 0:CC], identf[:]),
            "tpD", mark=("CCHt",))
        waitm("vector", ("CCHt",))
        opE("vector", lambda e: e.tensor_copy(Htr[0:CC, :], psS[0:CC, 0:128]), mark=("CCHe",))
        for ccx in range(CC):
            if ccx == 0:
                waitm("gpsimd", ("CCHe",))
            else:
                waitm("gpsimd", ("DHm", ccx - 1))
            dma("gpsimd", Rrow[0:1, :], Htr[ccx:ccx + 1, :], "ccd", mark=("DHr", ccx))
            waitm("tensor", ("DHr", ccx))
            bank = [psS, psS2][ccx % 2]
            if ccx >= 2:
                waitm("tensor", ("DHe", ccx - 2))
            opT(lambda e, bank=bank: e.matmul(
                bank[:, 0:128], ones_row[:], Rrow[0:1, :], start=True, stop=True),
                "mmB", mark=("DHm", ccx))
            waitm("vector", ("DHm", ccx))
            opE("vector", lambda e, ccx=ccx, bank=bank: e.tensor_copy(
                Hb[:, ccx * 128:(ccx + 1) * 128], bank[:, 0:128]),
                mark=("DHe", ccx))
        marks[("DHb",)] = marks[("DHe", CC - 1)]

        waitm("sync", ("AGu",))
        for n in range(NSL):
            qp = n % 2
            for (mat, wap, wqbs) in [("g", wgq, wqg), ("u", wuq, wqu)]:
                if n >= 2:
                    waitm("sync", ("C1mmN", n - 2))
                for k in range(KC):
                    dma("sync", wqbs[qp][:, k * 512:(k + 1) * 512],
                        wap[k * 128:(k + 1) * 128, n * 512:(n + 1) * 512], f"c1l{qp}")
                marks[("C1q", n, mat)] = (f"c1l{qp}", cnt[f"c1l{qp}"])
            for tb in range(NB):
                idx = n * NB + tb
                tp = idx % 2
                pg_ = psA if tp == 0 else psC
                pu_ = psB if tp == 0 else psD
                waitm("tensor", ("C1q", n, "u"))
                if idx >= 2:
                    waitm("tensor", ("C1sil", idx - 2))
                    waitm("tensor", ("C1hsv", idx - 2))
                for k in range(KC):
                    opT(lambda e, k=k, tb=tb, pg_=pg_, qp=qp: e.matmul(
                        pg_[:], ixfm[:, k * TOK + tb * 128: k * TOK + (tb + 1) * 128],
                        wqg[qp][:, k * 512:(k + 1) * 512], start=(k == 0), stop=(k == KC - 1)))
                    opT(lambda e, k=k, tb=tb, pu_=pu_, qp=qp: e.matmul(
                        pu_[:], ixfm[:, k * TOK + tb * 128: k * TOK + (tb + 1) * 128],
                        wqu[qp][:, k * 512:(k + 1) * 512], start=(k == 0), stop=(k == KC - 1)),
                        "mm1" if k == KC - 1 else None,
                        mark=("C1mmP", idx) if k == KC - 1 else None)
                if tb == NB - 1:
                    marks[("C1mmN", n)] = ("mm1", cnt["mm1"])
                waitm("scalar", ("C1mmP", idx))
                if idx >= 2:
                    waitm("scalar", ("C1hsv", idx - 2))
                opE("scalar", lambda e, tp=tp, tb=tb, pg_=pg_: e.activation(
                    sgate[tp][:], pg_[:], AF.Silu, scale=scg[:, tb:tb + 1]), mark=("C1sil", idx))
                waitm("vector", ("C1mmP", idx))
                waitm("vector", ("C1sil", idx))
                if idx >= 2:
                    waitm("vector", ("C1hw", idx - 2))
                opE("vector", lambda e, tp=tp, tb=tb, pu_=pu_: e.scalar_tensor_tensor(
                    hst[tp][:], pu_[:], scu[:, tb:tb + 1], sgate[tp][:], ALU.mult, ALU.mult),
                    mark=("C1hsv", idx))
                waitm("gpsimd", ("C1hsv", idx))
                dma("gpsimd", h_a[tb * 128:(tb + 1) * 128, n * 512:(n + 1) * 512],
                    hst[tp][:], f"hw{tp}", mark=("C1hw", idx))
        if barrier():
            return

        waitv("gpsimd", "hw0", cnt["hw0"])
        waitv("gpsimd", "hw1", cnt["hw1"])
        for tb in range(NB):
            hp = tb % 2
            if tb >= 2:
                waitm("sync", ("C2strips", tb - 2))
            dma("sync", hfull[hp][:], h_a[tb * 128:(tb + 1) * 128, :], f"hr{hp}", mark=("C2hr", tb))
            waitm("scalar", ("C2hr", tb))
            opE("scalar", lambda e, hp=hp, tb=tb: e.activation(
                sqh_scr[:], hfull[hp][:, :H // 2], AF.Square, accum_out=hsq1[:, tb:tb + 1]))
            opE("scalar", lambda e, hp=hp, tb=tb: e.activation(
                sqh_scr[:], hfull[hp][:, H // 2:], AF.Square, accum_out=hsq2[:, tb:tb + 1]),
                mark=("C2sq", tb))
            waitm("vector", ("C2hr", tb))
            opE("vector", lambda e, hp=hp, tb=tb: e.reduce_max(
                maxh[:, tb:tb + 1], hfull[hp][:], AX.X, apply_absolute_value=True))
            ts_ = slice(tb, tb + 1)
            waitm("vector", ("C2sq", tb))
            opE("vector", lambda e, ts_=ts_: e.tensor_tensor(
                sumsqh[:, ts_], hsq1[:, ts_], hsq2[:, ts_], ALU.add))
            opE("vector", lambda e, ts_=ts_: e.tensor_scalar(
                tmpa[:, ts_], sumsqh[:, ts_], 1.0 / H, None, ALU.mult), mark=("C2e", tb))
            waitm("scalar", ("C2e", tb))
            opE("scalar", lambda e, ts_=ts_: e.activation(
                tmpb[:, ts_], tmpa[:, ts_], AF.Sqrt, bias=c6[:]), mark=("C2sqh", tb))
            waitm("vector", ("C2sqh", tb))
            opE("vector", lambda e, ts_=ts_: e.reciprocal(tmpc[:, ts_], tmpb[:, ts_]))
            opE("vector", lambda e, ts_=ts_: e.tensor_tensor(tmpa[:, ts_], maxh[:, ts_], tmpc[:, ts_], ALU.mult))
            opE("vector", lambda e, ts_=ts_: e.tensor_scalar(tmpa[:, ts_], tmpa[:, ts_], 1e-5, None, ALU.max))
            opE("vector", lambda e, ts_=ts_: e.tensor_scalar(sinvh[:, ts_], tmpa[:, ts_], 1.0 / 127.0, None, ALU.mult))
            opE("vector", lambda e, ts_=ts_: e.reciprocal(tmpb[:, ts_], tmpa[:, ts_]))
            opE("vector", lambda e, ts_=ts_: e.tensor_tensor(tmpc[:, ts_], tmpc[:, ts_], tmpb[:, ts_], ALU.mult))
            opE("vector", lambda e, ts_=ts_: e.tensor_scalar(alphah[:, ts_], tmpc[:, ts_], 127.0, None, ALU.mult),
                mark=("C2a", tb))
            if tb >= 1 and "c2ihw" not in KSKIP:
                waitm("scalar", ("C2ihw", tb - 1))
            for st in (range(H // 512) if "c2st" not in KSKIP else []):
                bp = st % 2
                opE("vector", lambda e, hp=hp, st=st, tb=tb: e.tensor_scalar(
                    ybufH[:], hfull[hp][:, st * 512:(st + 1) * 512], alphah[:, tb:tb + 1], None, ALU.mult))
                opE("vector", lambda e: e.tensor_scalar(i8h[:], ybufH[:], 1.0, None, ALU.mult))
                opE("vector", lambda e: e.tensor_tensor(rbufH[:], ybufH[:], i8h[:], ALU.subtract))
                opE("vector", lambda e: e.tensor_scalar(cbufH[:], rbufH[:], 0.5, None, ALU.is_gt))
                opE("vector", lambda e: e.scalar_tensor_tensor(
                    ybufH[:], rbufH[:], -0.5, cbufH[:], ALU.is_lt, ALU.subtract))
                if st >= 2 and "c2tr" not in KSKIP:
                    waitm("vector", ("C2ev", tb, st - 2))
                opE("vector", lambda e, bp=bp: e.tensor_tensor(
                    ibfh[bp][:], i8h[:], ybufH[:], ALU.subtract),
                    mark=("C2ib", tb, st))
                if "c2tr" not in KSKIP:
                    pairsC = [(ibfh[bp][:, c4 * 128:(c4 + 1) * 128],
                               ihstage[:, (st * 4 + c4) * 128:(st * 4 + c4 + 1) * 128])
                              for c4 in range(4)]
                    marks[("C2ev", tb, st)] = trstream(
                        "C", "tpC", pairsC, ident[:], [ptrA, ptrB], 8, ("C2ib", tb, st),
                        ev="scalar")
            marks[("C2strips", tb)] = ("vch", cnt["vch"])
            if "c2st" not in KSKIP and "c2tr" not in KSKIP:
                marks[("C2evall", tb)] = marks[("C2ev", tb, H // 512 - 1)]
            elif "c2tr" in KSKIP:
                marks[("C2evall", tb)] = ("vch", cnt["vch"])
            if "c2ihw" not in KSKIP:
                waitm("gpsimd", ("C2evall", tb) if "c2st" not in KSKIP else ("C2a", tb))
                for hkg in range(HKG):
                    dma("gpsimd", ih_a[(hkg * NB + tb) * 128:(hkg * NB + tb + 1) * 128, :],
                        ihstage[:, hkg * 1024:(hkg + 1) * 1024], f"ihw{tb % 2}")
            marks[("C2ihw", tb)] = (f"ihw{tb % 2}", cnt[f"ihw{tb % 2}"])
        if barrier():
            return

        waitv("gpsimd", "ihw0", cnt["ihw0"])
        waitv("gpsimd", "ihw1", cnt["ihw1"])
        waitm("sync", ("AGw",))
        for hkg in range(HKG):
            wp = hkg % 2
            if hkg >= 2:
                waitm("sync", ("C3mm", hkg - 2))
            for hk2 in range(8):
                dma("sync", wdq[wp][:, hk2 * D:(hk2 + 1) * D],
                    wdqa[(hkg * 8 + hk2) * 128:(hkg * 8 + hk2 + 1) * 128, :],
                    f"wdl{wp}", mark=("C3q", hkg, hk2))
            for tb in range(NB):
                idx = hkg * NB + tb
                ip = idx % 2
                if idx >= 2:
                    waitm("scalar", ("C3mmI", idx - 2))
                dma("scalar", ihsb[ip][:], ih_a[(hkg * NB + tb) * 128:(hkg * NB + tb + 1) * 128, :],
                    f"ihr{ip}", mark=("C3ihr", idx))
                for ds in range(NDS):
                    yp = (idx * NDS + ds) % 2
                    pys = [psA[:], psB[:]] if yp == 0 else [psC[:], psD[:]]
                    waitm("tensor", ("C3ihr", idx))
                    waitm("tensor", ("C3q", hkg, 7))
                    if (idx * NDS + ds) >= 2:
                        waitm("tensor", ("C3yac", idx * NDS + ds - 2))
                    nhalf = 2 if D >= 1024 else 1
                    for k2 in range(8):
                        for hf in range(nhalf):
                            c0 = ds * 1024 + hf * DW
                            opT(lambda e, ip=ip, k2=k2, hf=hf, wp=wp, c0=c0, pys=pys: e.matmul(
                                pys[hf][:, :DW], ihsb[ip][:, k2 * 128:(k2 + 1) * 128],
                                wdq[wp][:, k2 * D + c0: k2 * D + c0 + DW],
                                start=(k2 == 0), stop=(k2 == 7)),
                                "mm3" if (k2 == 7 and hf == nhalf - 1) else None,
                                mark=("C3mmP", idx, ds) if (k2 == 7 and hf == nhalf - 1) else None)
                    waitm("vector", ("C3mmP", idx, ds))
                    for hf in range(nhalf):
                        c0 = tb * D + ds * 1024 + hf * DW
                        if hkg == 0:
                            opE("vector", lambda e, c0=c0, hf=hf, pys=pys: e.tensor_copy(
                                yacc[:, c0:c0 + DW], pys[hf][:, :DW]),
                                mark=("C3yac", idx * NDS + ds) if hf == nhalf - 1 else None)
                        else:
                            opE("vector", lambda e, c0=c0, hf=hf, pys=pys: e.tensor_tensor(
                                yacc[:, c0:c0 + DW], yacc[:, c0:c0 + DW], pys[hf][:, :DW], ALU.add),
                                mark=("C3yac", idx * NDS + ds) if hf == nhalf - 1 else None)
                marks[("C3mmI", idx)] = ("vch", cnt["vch"])
            marks[("C3mm", hkg)] = ("mm3", cnt["mm3"])
        if barrier():
            return

        waitv("gpsimd", "sw0", cnt["sw0"])
        waitv("gpsimd", "sw1", cnt["sw1"])
        opE("vector", lambda e: e.tensor_scalar(scd[:, nbs], sinvh[:, nbs], wclip[:, 2:3], -1.0, ALU.mult, ALU.mult),
            mark=("Dscd",))
        for tb in range(NB):
            p = tb % 2
            if tb >= 2:
                waitm("sync", ("Dsg", tb - 2))
            dma("sync", s_tm[p][:], s_a[tb * 128:(tb + 1) * 128, :], f"sr{p}")
            dma("sync", p_tm[p][:], p_a[tb * 128:(tb + 1) * 128, :], f"sr{p}",
                mark=("Dsr", tb))
            if tb >= 2:
                waitm("scalar", ("Dob", tb - 2))
            dma("scalar", xod[p][:], xo[tb * 128:(tb + 1) * 128, :], f"xol{p}", mark=("Dxo", tb))
            waitm("scalar", ("Dscd",))
            if tb >= 1:
                waitm("scalar", ("Dsg", tb - 1))
            opE("scalar", lambda e, tb=tb: e.activation(
                gfin[:], yacc[:, tb * D:(tb + 1) * D], AF.Silu, scale=scd[:, tb:tb + 1]),
                mark=("Dgf", tb))
            waitm("vector", ("Dgf", tb))
            waitm("vector", ("Dsr", tb))
            opE("vector", lambda e, p=p: e.tensor_tensor(sgb[:], p_tm[p][:], Hb[:, 0:E], ALU.mult))
            opE("vector", lambda e, p=p: e.tensor_tensor(s_tm[p][:], s_tm[p][:], sgb[:], ALU.add))
            opE("vector", lambda e, p=p: e.tensor_tensor(sgb[:], s_tm[p][:], gfin[:], ALU.mult),
                mark=("Dsg", tb))
            waitm("vector", ("Dxo", tb))
            if tb >= 2:
                waitm("vector", ("Dout", tb - 2))
            opE("vector", lambda e, p=p: e.tensor_tensor(outsb[p][:], sgb[:], xod[p][:], ALU.add),
                mark=("Dob", tb))
            waitm("gpsimd", ("Dob", tb))
            dma("gpsimd", out[tb * 128:(tb + 1) * 128, :], outsb[p][:], f"outw{p}", mark=("Dout", tb))
        waitv("gpsimd", "outw0", cnt["outw0"])
        waitv("gpsimd", "outw1", cnt["outw1"])

    from contextlib import ExitStack as ES2
    with nc.Block() as block:
        @block.sync
        def _(e):
            describe("sync", e, {})

        @block.vector
        def _(e):
            describe("vector", e, {})

        @block.scalar
        def _(e):
            describe("scalar", e, {})

        @block.gpsimd
        def _(e):
            describe("gpsimd", e, {})

        @block.tensor
        def _(e):
            describe("tensor", e, {})

    stack.close()
    return nc


_NC_CACHE = {}


def _get_nc(S, D, H):
    key = (S, D, H)
    if key not in _NC_CACHE:
        _NC_CACHE[key] = build_nc(S, D, H)
    return _NC_CACHE[key]


def _pair_permute_wt(W, E):
    WT = np.ascontiguousarray(W.T)
    cols = []
    for cc in range(E // 128):
        cols.append(WT[:, cc * 128:(cc + 1) * 128])
        cols.append(WT[:, E + cc * 128: E + (cc + 1) * 128])
    return np.ascontiguousarray(np.concatenate(cols, axis=1))


def run(x, W, wgate, wup, wdown):
    B, S, D = x.shape
    H = wgate.shape[0]
    E = D
    nc = _get_nc(S, D, H)
    x = np.ascontiguousarray(np.asarray(x, np.float32))
    wtp = _pair_permute_wt(np.asarray(W, np.float32), E)
    wgt = np.ascontiguousarray(np.asarray(wgate, np.float32).T)
    wut = np.ascontiguousarray(np.asarray(wup, np.float32).T)
    wdt = np.ascontiguousarray(np.asarray(wdown, np.float32).T)
    TOK = S // 4
    in_maps = []
    for cid in range(8):
        b, j = cid // 4, cid % 4
        m = np.zeros(16, np.float32)
        for r in range(8):
            m[r] = 1.0 if (r // 4 == b and r % 4 < j) else 0.0
            m[8 + r] = 1.0 - m[r]
        shw = wtp.shape[0] // 8
        shd = wdt.shape[0] // 8
        in_maps.append({
            "xown": np.ascontiguousarray(x[b, j * TOK:(j + 1) * TOK]),
            "wtsh": np.ascontiguousarray(wtp[cid * shw:(cid + 1) * shw]),
            "wgsh": np.ascontiguousarray(wgt[cid * shw:(cid + 1) * shw]),
            "wush": np.ascontiguousarray(wut[cid * shw:(cid + 1) * shw]),
            "wdsh": np.ascontiguousarray(wdt[cid * shd:(cid + 1) * shd]),
            "cmask": np.ascontiguousarray(np.tile(m, (128, 1))),
        })
    res = run_bass_kernel_spmd(nc, in_maps, core_ids=list(range(8)))
    outs = [np.asarray(res.results[c]["out"]) for c in range(8)]
    full = np.stack([np.concatenate(outs[0:4], axis=0),
                     np.concatenate(outs[4:8], axis=0)], axis=0)
    return full.astype(np.float32)


def kernel(L0_attn_x, L0_attn_W, L0_ffn_w_gate, L0_ffn_w_up, L0_ffn_w_down):
    return run(L0_attn_x, L0_attn_W, L0_ffn_w_gate, L0_ffn_w_up, L0_ffn_w_down)
